# version 29
# baseline (speedup 1.0000x reference)
import numpy as np
import concourse.bass as bass
import concourse.mybir as mybir
from concourse.bass_utils import run_bass_kernel_spmd

F32 = mybir.dt.float32
BF16 = mybir.dt.bfloat16
I32 = mybir.dt.int32
U8 = mybir.dt.uint8
AF = mybir.ActivationFunctionType
ALU = mybir.AluOpType
AXX = mybir.AxisListType.X

NCORES = 8
EPS = 1e-5

FULL = dict(T=2048, D=2048, R=2048, DH=512, I=8192)


def build_nc(T=2048, D=2048, R=2048, DH=512, I=8192):
    TP = T + 128
    LT = T // 128
    DC = D // 128
    RC = R // 128
    DHC = DH // 128
    IC = I // 128
    NL = min(512, T); NLC = T // NL
    NQC = NLC
    ND = min(512, D); NDC = D // ND
    LH = 2
    TH = T // LH
    LTH = TH // 128
    invD = 1.0 / D

    nc = bass.Bass("TRN2", target_bir_lowering=False)

    x_pad = nc.dram_tensor("x_pad", [TP, D], F32, kind="ExternalInput").ap()
    wuT = nc.dram_tensor("wuT", [D, R], F32, kind="ExternalInput").ap()
    wvT = nc.dram_tensor("wvT", [D, R], F32, kind="ExternalInput").ap()
    wg1T = nc.dram_tensor("wg1T", [D, DH], F32, kind="ExternalInput").ap()
    wg2T = nc.dram_tensor("wg2T", [DH, R], F32, kind="ExternalInput").ap()
    bu_r = nc.dram_tensor("bu_r", [128, RC], F32, kind="ExternalInput").ap()
    bv_r = nc.dram_tensor("bv_r", [128, RC], F32, kind="ExternalInput").ap()
    bg1_r = nc.dram_tensor("bg1_r", [128, DHC], F32, kind="ExternalInput").ap()
    bg2_r = nc.dram_tensor("bg2_r", [128, RC], F32, kind="ExternalInput").ap()
    convw_r = nc.dram_tensor("convw_r", [128, 3 * DC], F32, kind="ExternalInput").ap()
    convb_r = nc.dram_tensor("convb_r", [128, DC], F32, kind="ExternalInput").ap()
    gateT = nc.dram_tensor("gateT", [D, I], U8, kind="ExternalInput").ap()
    upT = nc.dram_tensor("upT", [D, I], U8, kind="ExternalInput").ap()
    downT = nc.dram_tensor("downT", [I, D], U8, kind="ExternalInput").ap()
    gs_r = nc.dram_tensor("gs_r", [128, IC], F32, kind="ExternalInput").ap()
    us_r = nc.dram_tensor("us_r", [128, IC], F32, kind="ExternalInput").ap()
    sd_b = nc.dram_tensor("sd_b", [128, D], F32, kind="ExternalInput").ap()
    premask = nc.dram_tensor("premask", [128, NCORES], F32, kind="ExternalInput").ap()
    ident_in = nc.dram_tensor("ident_in", [128, 128], F32, kind="ExternalInput").ap()

    out_d = nc.dram_tensor("out_d", [T, D], F32, kind="ExternalOutput").ap()

    x1tm_d = nc.dram_tensor("x1tm_d", [TP, D], BF16, kind="Internal").ap()
    x2tm_d = nc.dram_tensor("x2tm_d", [T, D], BF16, kind="Internal").ap()
    C_d = nc.dram_tensor("C_d", [RC, 128, T], BF16, kind="Internal").ap()
    Vg_d = nc.dram_tensor("Vg_d", [RC, 128, T], BF16, kind="Internal").ap()
    t_d = nc.dram_tensor("t_d", [DC, 128, T], F32, kind="Internal").ap()
    xp_d = nc.dram_tensor("xp_d", [T, D], F32, kind="Internal").ap()
    hf_d = nc.dram_tensor("hf_d", [IC, 128, T], BF16, kind="Internal").ap()
    cc_in = nc.dram_tensor("cc_in", [1, R], F32, kind="Internal").ap()
    cc_out = nc.dram_tensor("cc_out", [NCORES, R], F32, kind="Internal").ap()

    f32b = lambda n: (4 * n + 31) // 32 * 32
    bfb = lambda n: (2 * n + 31) // 32 * 32
    BASE = (nc._sbuf_addr_for_side("left") + 31) // 32 * 32
    TOPA = nc._sbuf_addr_for_side("right")
    LIMIT = TOPA - BASE - 3584
    nc.alloc_sbuf_tensor("manual_guard", [128, LIMIT], mybir.dt.uint8)
    _dtb = {F32: 4, BF16: 2, I32: 4, U8: 1}

    def at(name, shape, dtype, offset):
        offset = BASE + (offset + 31) // 32 * 32
        assert offset + shape[-1] * _dtb[dtype] <= BASE + LIMIT, name
        return nc.alloc_sbuf_tensor_at(name, shape, dtype, offset=offset)

    p = 0
    bu_sb = at("bu_sb", [128, RC], F32, offset=p); p += f32b(RC)
    bv_sb = at("bv_sb", [128, RC], F32, offset=p); p += f32b(RC)
    bg1_sb = at("bg1_sb", [128, DHC], F32, offset=p); p += f32b(DHC)
    bg2_sb = at("bg2_sb", [128, RC], F32, offset=p); p += f32b(RC)
    convw_sb = at("convw_sb", [128, 3 * DC], F32, offset=p); p += f32b(3 * DC)
    convb_sb = at("convb_sb", [128, DC], F32, offset=p); p += f32b(DC)
    gs_sb = at("gs_sb", [128, IC], F32, offset=p); p += f32b(IC)
    us_sb = at("us_sb", [128, IC], F32, offset=p); p += f32b(IC)
    sd_sb = at("sd_sb", [128, D], F32, offset=p); p += f32b(D)
    pm_sb = at("pm_sb", [128, NCORES], F32, offset=p); p += f32b(NCORES)
    ident = at("ident", [128, 128], F32, offset=p); p += f32b(128)
    ones_sb = at("ones_sb", [128, T], F32, offset=p); p += f32b(T)
    S_sb = at("S_sb", [128, RC], F32, offset=p); p += f32b(RC)
    O_sb = at("O_sb", [128, RC], F32, offset=p); p += f32b(RC)
    ag_sb = at("ag_sb", [128, NCORES * RC], F32, offset=p); p += f32b(NCORES * RC)
    stat = at("stat", [128, 12], F32, offset=p); p += f32b(12)
    cvals = at("cvals", [128, 2], F32, offset=p); p += f32b(2)
    PERM_END = (p + 63) // 64 * 64
    assert PERM_END <= 24 * 1024, PERM_END

    X1W = TP
    x1fm = [at(f"x1fm{c}", [128, X1W], BF16, offset=PERM_END + c * bfb(X1W)) for c in range(DC)]
    X1_END = PERM_END + DC * bfb(X1W)
    hfm = [at(f"hfm{c}", [128, T], BF16, offset=X1_END + c * bfb(T)) for c in range(DHC)]
    H_END = X1_END + DHC * bfb(T)

    p = H_END
    xin = [at(f"xin{j}", [128, D], F32, offset=p + j * f32b(D)) for j in range(2)]; p += 2 * f32b(D)
    sq_sb = at("sq_sb", [128, D], F32, offset=p); p += f32b(D)
    x1tm = [at(f"x1tm{j}", [128, D], BF16, offset=p + j * bfb(D)) for j in range(2)]; p += 2 * bfb(D)
    assert p <= LIMIT, p

    p = H_END
    wstg = [at(f"wstg{j}", [128, 2 * D], F32, offset=p + j * f32b(2 * D)) for j in range(2)]; p += 2 * f32b(2 * D)
    wubf = [at(f"wubf{j}", [128, D], BF16, offset=p + j * bfb(D)) for j in range(2)]; p += 2 * bfb(D)
    wvbf = [at(f"wvbf{j}", [128, D], BF16, offset=p + j * bfb(D)) for j in range(2)]; p += 2 * bfb(D)
    wg2stg = [at(f"wg2stg{j}", [128, DH], F32, offset=p + j * f32b(DH)) for j in range(2)]; p += 2 * f32b(DH)
    wg2bf = [at(f"wg2bf{j}", [128, DH], BF16, offset=p + j * bfb(DH)) for j in range(2)]; p += 2 * bfb(DH)
    g_sb = at("g_sb", [128, T], F32, offset=p); p += f32b(T)
    ug_sb = at("ug_sb", [128, T], F32, offset=p); p += f32b(T)
    wg1bf = [at(f"wg1bf{c}", [128, D], BF16, offset=p + c * bfb(D)) for c in range(DHC)]
    C_sb = [at(f"C_sb{j}", [128, T], BF16, offset=p + j * bfb(T)) for j in range(2)]; p += 2 * bfb(T)
    Vg_sb = [at(f"Vg_sb{j}", [128, T], BF16, offset=p + j * bfb(T)) for j in range(2)]; p += 2 * bfb(T)
    tfm0 = at("tfm0", [128, T], F32, offset=p)
    TFM_END = p + f32b(T)
    assert TFM_END <= LIMIT, p
    assert DHC * bfb(D) <= 2 * f32b(T) + 2 * bfb(T)

    p = H_END
    wun_stg = [at(f"wun_stg{j}", [128, D], F32, offset=p + j * f32b(D)) for j in range(2)]; p += 2 * f32b(D)
    WUN_START = p
    wun_bf = [at(f"wun_bf{c}", [128, D], BF16, offset=p + c * bfb(D)) for c in range(RC)]; p += RC * bfb(D)
    assert p <= LIMIT, p
    PH3_END = p

    NQ = min(512, T)
    p = PH3_END
    x2q0 = [at(f"x2q0{c}", [128, NQ], BF16, offset=p + c * bfb(NQ)) for c in range(DC)]; p += DC * bfb(NQ)
    gstg0_e = at("gstg0_e", [128, D], U8, offset=p); p += (D + 31) // 32 * 32
    ustg0_e = at("ustg0_e", [128, D], U8, offset=p); p += (D + 31) // 32 * 32
    assert p <= LIMIT, p

    sz4 = (3 * bfb(R) + 3 * bfb(R) + 2 * f32b(D) + 2 * f32b(D) + 2 * bfb(R)
           + f32b(D) + 2 * bfb(D))
    p = PERM_END if PERM_END + sz4 <= WUN_START else max(PH3_END, TFM_END)
    Cin = [at(f"Cin{j}", [128, R], BF16, offset=p + j * bfb(R)) for j in range(3)]; p += 3 * bfb(R)
    Vgin = [at(f"Vgin{j}", [128, R], BF16, offset=p + j * bfb(R)) for j in range(3)]; p += 3 * bfb(R)
    tin = [at(f"tin{j}", [128, D], F32, offset=p + j * f32b(D)) for j in range(2)]; p += 2 * f32b(D)
    xin4 = [at(f"xin4{j}", [128, D], F32, offset=p + j * f32b(D)) for j in range(2)]; p += 2 * f32b(D)
    G_sb = [at(f"G_sb{j}", [128, R], BF16, offset=p + j * bfb(R)) for j in range(2)]; p += 2 * bfb(R)
    xp_sb = at("xp_sb", [128, D], F32, offset=p); p += f32b(D)
    x2tm = [at(f"x2tm{j}", [128, D], BF16, offset=p + j * bfb(D)) for j in range(2)]; p += 2 * bfb(D)
    assert p <= LIMIT and (p <= WUN_START or p > PH3_END), (p, WUN_START, PH3_END)

    p = PERM_END
    u8b = lambda n: (n + 31) // 32 * 32
    x2fm = [at(f"x2fm{c}", [128, T], BF16, offset=p + c * bfb(T)) for c in range(DC)]; p += DC * bfb(T)
    gstg = [gstg0_e, at("gstg1", [128, D], U8, offset=p)]; p += u8b(D)
    ustg = [ustg0_e, at("ustg1", [128, D], U8, offset=p)]; p += u8b(D)
    gbf = [at(f"gbf{j}", [128, D], BF16, offset=p + j * bfb(D)) for j in range(2)]; p += 2 * bfb(D)
    ubf = [at(f"ubf{j}", [128, D], BF16, offset=p + j * bfb(D)) for j in range(2)]; p += 2 * bfb(D)
    silu_sb = at("silu_sb", [128, T], F32, offset=p); p += f32b(T)
    hf_sb = [at(f"hf_sb{j}", [128, T], BF16, offset=p + j * bfb(T)) for j in range(2)]; p += 2 * bfb(T)
    assert p <= LIMIT, p

    p = PERM_END
    hfh = [at(f"hfh{c}", [128, TH], BF16, offset=p + c * bfb(TH)) for c in range(IC)]; p += IC * bfb(TH)
    dstg = [at(f"dstg{j}", [128, ND], U8, offset=p + j * u8b(ND)) for j in range(4)]; p += 4 * u8b(ND)
    dbf = [at(f"dbf{j}", [128, ND], BF16, offset=p + j * bfb(ND)) for j in range(4)]; p += 4 * bfb(ND)
    xr_sb = [at(f"xr_sb{j}", [128, LTH * ND], F32, offset=p + j * f32b(LTH * ND)) for j in range(2)]; p += 2 * f32b(LTH * ND)
    outb = [at(f"outb{j}", [128, ND], F32, offset=p + j * f32b(ND)) for j in range(4)]; p += 4 * f32b(ND)
    assert p <= LIMIT, p

    psA = nc.alloc_psum_tensor("psA", [128, 2048], F32)
    psB = nc.alloc_psum_tensor("psB", [128, 2048], F32)
    psall = [psA, psB]

    S = {}
    for n in ("pl0", "xld0", "xld1", "dve1", "act1", "x1sp0", "x1sp1",
              "xb1", "wg1ld", "wg1ld1", "pool_h", "pe_h", "act_h",
              "wld0", "wld1", "pool2", "pe2", "act2", "dve2", "sp20", "sp21",
              "wuld0", "wuld1", "pool3", "act3", "dve3", "tsp0", "tsp1",
              "ccsp", "cc", "agld", "dve3b",
              "ld4a0", "ld4a1", "ld4a2", "ld4v0", "ld4v1", "ld4v2",
              "tld0", "tld1", "ld4b0", "ld4b1", "dve4", "act4", "pe4", "sp40", "sp41",
              "xbq0", "xbq1", "xbq3", "xb123", "dve2v",
              "gld0", "gld1", "pool5", "pe5", "act5", "dve5", "hsp0", "hsp1",
              "hga", "hgb", "dld0", "dld1", "dld2", "dld3", "pool6", "pe6", "dve6", "xrld", "xrld1", "osp0", "osp1", "osp2", "osp3"):
        S[n] = nc.alloc_semaphore(n)
    for c in range(D // 128):
        S[f"xbs{c}"] = nc.alloc_semaphore(f"xbs{c}")

    with nc.Block() as blk:
        @blk.sync
        def _(sync):
            sync.dma_start(out=bu_sb[:, :], in_=bu_r[:, :]).then_inc(S["pl0"], 16)
            sync.dma_start(out=bv_sb[:, :], in_=bv_r[:, :]).then_inc(S["pl0"], 16)
            sync.dma_start(out=bg1_sb[:, :], in_=bg1_r[:, :]).then_inc(S["pl0"], 16)
            sync.dma_start(out=bg2_sb[:, :], in_=bg2_r[:, :]).then_inc(S["pl0"], 16)
            sync.dma_start(out=convw_sb[:, :], in_=convw_r[:, :]).then_inc(S["pl0"], 16)
            sync.dma_start(out=convb_sb[:, :], in_=convb_r[:, :]).then_inc(S["pl0"], 16)
            sync.dma_start(out=gs_sb[:, :], in_=gs_r[:, :]).then_inc(S["pl0"], 16)
            sync.dma_start(out=us_sb[:, :], in_=us_r[:, :]).then_inc(S["pl0"], 16)
            sync.dma_start(out=sd_sb[:, :], in_=sd_b[:, :]).then_inc(S["pl0"], 16)
            sync.dma_start(out=pm_sb[:, :], in_=premask[:, :]).then_inc(S["pl0"], 16)
            sync.dma_start(out=ident[:, :], in_=ident_in[:, :]).then_inc(S["pl0"], 16)
            sync.wait_ge(S["pl0"], 16 * 11)

        @blk.vector
        def _(vector):
            vector.memset(ones_sb[:, :], 1.0)
            vector.memset(O_sb[:, :], 0.0)
            vector.memset(cvals[:, 0:1], EPS)
            vector.memset(cvals[:, 1:2], invD)

    NT1 = TP // 128
    with nc.Block() as blk:
        @blk.sync
        def _(sync):
            for t in range(NT1):
                pr = t % 2
                if t >= 2:
                    sync.wait_ge(S["dve1"], 4 * (t - 2) + 4)
                sync.dma_start(out=xin[pr][:, :],
                               in_=x_pad[128 * t:128 * (t + 1), :]).then_inc(S["xld" + str(pr)], 16)

        @blk.vector
        def _(vector):
            for t in range(NT1):
                pr = t % 2
                vector.wait_ge(S["xld" + str(pr)], 16 * (t // 2 + 1))
                if t >= 1:
                    vector.wait_ge(S["act1"], 3 * (t - 1) + 1)
                if t >= 1:
                    vector.wait_ge(S["dve1"], 4 * t)
                vector.reduce_sum(stat[:, 0:1], xin[pr][:, :], axis=AXX).then_inc(S["dve1"], 1)
                vector.wait_ge(S["act1"], 3 * t + 3)
                vector.reciprocal(stat[:, 4:5], stat[:, 3:4]).then_inc(S["dve1"], 1)
                vector.wait_ge(S["dve1"], 4 * t + 2)
                vector.tensor_mul(stat[:, 5:6], stat[:, 1:2], stat[:, 4:5]).then_inc(S["dve1"], 1)
                if t >= 2:
                    vector.wait_ge(S["x1sp" + str(pr)], 16 * (t // 2))
                vector.wait_ge(S["dve1"], 4 * t + 3)
                vector.tensor_scalar(out=x1tm[pr][:, :], in0=xin[pr][:, :],
                                     scalar1=stat[:, 4:5], scalar2=stat[:, 5:6],
                                     op0=ALU.mult, op1=ALU.add).then_inc(S["dve1"], 1)

        @blk.scalar
        def _(scalar):
            for t in range(NT1):
                pr = t % 2
                scalar.wait_ge(S["dve1"], 4 * t + 1)
                if t >= 1:
                    scalar.wait_ge(S["act1"], 3 * t)
                scalar.mul(stat[:, 1:2], stat[:, 0:1], -invD).then_inc(S["act1"], 1)
                scalar.wait_ge(S["act1"], 3 * t + 1)
                scalar.activation(out=sq_sb[:, :], in_=xin[pr][:, :], func=AF.Square,
                                  bias=stat[:, 1:2], accum_out=stat[:, 2:3]).then_inc(S["act1"], 1)
                scalar.wait_ge(S["act1"], 3 * t + 2)
                scalar.activation(out=stat[:, 3:4], in_=stat[:, 2:3], func=AF.Sqrt,
                                  scale=cvals[:, 1:2], bias=cvals[:, 0:1]).then_inc(S["act1"], 1)
                scalar.wait_ge(S["dve1"], 4 * t + 4)
                scalar.dma_start(out=x1tm_d[128 * t:128 * (t + 1), :],
                                 in_=x1tm[pr][:, :]).then_inc(S["x1sp" + str(pr)], 16)

    with nc.Block() as blk:
        @blk.sync
        def _(sync):
            for c in range(DHC):
                pr = c % 2
                if c >= 2:
                    sync.wait_ge(S["pool_h"], c - 1)
                sync.dma_start(
                    out=wstg[pr][:, 0:D].rearrange("p (c i) -> p c i", c=DC),
                    in_=wg1T[:, 128 * c:128 * (c + 1)].rearrange("(c p) i -> p c i", p=128),
                ).then_inc(S["wg1ld" if pr == 0 else "wg1ld1"], 16)
            H1T = NT1 // 2
            H1 = 128 * H1T
            sync.wait_ge(S["x1sp0"], 16 * ((H1T + 1) // 2))
            sync.wait_ge(S["x1sp1"], 16 * (H1T // 2))
            for c in range(DC):
                sync.dma_start(out=x1fm[c][:, 0:H1], in_=x1tm_d[0:H1, 128 * c:128 * (c + 1)],
                               transpose=True).then_inc(S[f"xbs{c}"], 16)
            sync.wait_ge(S["x1sp0"], 16 * ((NT1 + 1) // 2))
            sync.wait_ge(S["x1sp1"], 16 * (NT1 // 2))
            for c in range(DC):
                sync.dma_start(out=x1fm[c][:, H1:TP], in_=x1tm_d[H1:TP, 128 * c:128 * (c + 1)],
                               transpose=True).then_inc(S[f"xbs{c}"], 16)

        @blk.vector
        def _(vector):
            for c in range(DHC):
                pr = c % 2
                vector.wait_ge(S["wg1ld" if pr == 0 else "wg1ld1"], 16 * (c // 2 + 1))
                vector.tensor_copy(out=wg1bf[c][:, :], in_=wstg[pr][:, 0:D]).then_inc(S["pool_h"], 1)

        @blk.tensor
        def _(tensor):
            for c in range(DHC):
                ps = psA if c % 2 == 0 else psB
                tensor.wait_ge(S["pool_h"], c + 1)
                if c >= 2:
                    tensor.wait_ge(S["act_h"], c - 1)
                for n in range(NLC):
                    for k in range(DC):
                        if c == 0 and n == 0:
                            tensor.wait_ge(S[f"xbs{k}"], 32)
                        inst = tensor.matmul(ps[:, NL * n:NL * (n + 1)],
                                             wg1bf[c][:, 128 * k:128 * (k + 1)],
                                             x1fm[k][:, 2 + NL * n:2 + NL * (n + 1)],
                                             start=(k == 0), stop=(k == DC - 1))
                        if k == DC - 1:
                            inst.then_inc(S["pe_h"], 1)

        @blk.scalar
        def _(scalar):
            for c in range(DHC):
                ps = psA if c % 2 == 0 else psB
                scalar.wait_ge(S["pe_h"], NLC * (c + 1))
                scalar.activation(out=hfm[c][:, :], in_=ps[:, 0:T], func=AF.Relu,
                                  bias=bg1_sb[:, c:c + 1]).then_inc(S["act_h"], 1)

    with nc.Block() as blk:
        @blk.sync
        def _(sync):
            for rc in range(RC):
                pr = rc % 2
                if rc >= 2:
                    sync.wait_ge(S["pool2"], 3 * (rc - 2) + 3)
                sync.dma_start(
                    out=wstg[pr][:, 0:D].rearrange("p (c i) -> p c i", c=DC),
                    in_=wuT[:, 128 * rc:128 * (rc + 1)].rearrange("(c p) i -> p c i", p=128),
                ).then_inc(S["wld" + str(pr)], 16)
                sync.dma_start(
                    out=wstg[pr][:, D:2 * D].rearrange("p (c i) -> p c i", c=DC),
                    in_=wvT[:, 128 * rc:128 * (rc + 1)].rearrange("(c p) i -> p c i", p=128),
                ).then_inc(S["wld" + str(pr)], 16)
                sync.dma_start(
                    out=wg2stg[pr][:, :].rearrange("p (c i) -> p c i", c=DHC),
                    in_=wg2T[:, 128 * rc:128 * (rc + 1)].rearrange("(c p) i -> p c i", p=128),
                ).then_inc(S["wld" + str(pr)], 16)
            sync.wait_ge(S["dve2"], 2 * (RC - 1) + 1)
            sync.dma_start(out=cc_in.rearrange("o (p c) -> (o p) c", p=128),
                           in_=S_sb[:, :]).then_inc(S["ccsp"], 16)

        @blk.tensor
        def _(tensor):
            for rc in range(RC):
                pr = rc % 2
                tensor.wait_ge(S["pool2"], 3 * rc + 3)
                for n in range(NLC):
                    if rc >= 1:
                        tensor.wait_ge(S["dve2v"], 4 * (rc - 1) + n + 1)
                    for k in range(DHC):
                        inst = tensor.matmul(psA[:, NL * n:NL * (n + 1)],
                                             wg2bf[pr][:, 128 * k:128 * (k + 1)],
                                             hfm[k][:, NL * n:NL * (n + 1)],
                                             start=(k == 0), stop=(k == DHC - 1))
                        if n == NLC - 1 and k == DHC - 1:
                            inst.then_inc(S["pe2"], 1)
                if rc >= 1:
                    tensor.wait_ge(S["dve2"], 2 * (rc - 1) + 1)
                for n in range(NLC):
                    for k in range(DC):
                        inst = tensor.matmul(psB[:, NL * n:NL * (n + 1)],
                                             wubf[pr][:, 128 * k:128 * (k + 1)],
                                             x1fm[k][:, 2 + NL * n:2 + NL * (n + 1)],
                                             start=(k == 0), stop=(k == DC - 1))
                        if n == NLC - 1 and k == DC - 1:
                            inst.then_inc(S["pe2"], 1)
                tensor.wait_ge(S["act2"], rc + 1)
                for n in range(NLC):
                    for k in range(DC):
                        inst = tensor.matmul(psA[:, NL * n:NL * (n + 1)],
                                             wvbf[pr][:, 128 * k:128 * (k + 1)],
                                             x1fm[k][:, 2 + NL * n:2 + NL * (n + 1)],
                                             start=(k == 0), stop=(k == DC - 1))
                        if n == NLC - 1 and k == DC - 1:
                            inst.then_inc(S["pe2"], 1)

        @blk.scalar
        def _(scalar):
            for rc in range(RC):
                pr = rc % 2
                scalar.wait_ge(S["pe2"], 3 * rc + 1)
                if rc >= 1:
                    scalar.wait_ge(S["dve2v"], 4 * (rc - 1) + 4)
                scalar.activation(out=g_sb[:, :], in_=psA[:, 0:T], func=AF.Sigmoid,
                                  bias=bg2_sb[:, rc:rc + 1]).then_inc(S["act2"], 1)
                scalar.wait_ge(S["dve2"], 2 * rc + 2)
                scalar.dma_start(out=C_d[rc, :, :], in_=C_sb[pr][:, :]).then_inc(S["sp2" + str(pr)], 16)
                scalar.wait_ge(S["dve2v"], 4 * rc + 4)
                scalar.dma_start(out=Vg_d[rc, :, :], in_=Vg_sb[pr][:, :]).then_inc(S["sp2" + str(pr)], 16)
                if 1 <= rc <= DC - 1:
                    scalar.wait_ge(S["dve3"], 3 * (rc - 1) + 3)
                    scalar.dma_start(out=t_d[rc - 1, :, :],
                                     in_=tfm0[:, :]).then_inc(S["tsp0"], 16)
            scalar.wait_ge(S["sp20"], 32 * ((RC + 1) // 2))
            scalar.wait_ge(S["sp21"], 32 * (RC // 2))

        @blk.gpsimd
        def _(gpsimd):
            gpsimd.wait_ge(S["ccsp"], 16)
            gpsimd.collective_compute(
                "AllGather", ALU.bypass,
                replica_groups=[list(range(NCORES))],
                ins=[cc_in[:, :]], outs=[cc_out[:, :]],
            ).then_inc(S["cc"])

        @blk.vector
        def _(vector):
            def cast_w(rc2):
                pr2 = rc2 % 2
                vector.wait_ge(S["wld" + str(pr2)], 48 * (rc2 // 2 + 1))
                if rc2 >= 2:
                    vector.wait_ge(S["pe2"], 3 * (rc2 - 2) + 3)
                vector.tensor_copy(out=wubf[pr2][:, :], in_=wstg[pr2][:, 0:D]).then_inc(S["pool2"], 1)
                vector.tensor_copy(out=wvbf[pr2][:, :], in_=wstg[pr2][:, D:2 * D]).then_inc(S["pool2"], 1)
                vector.tensor_copy(out=wg2bf[pr2][:, :], in_=wg2stg[pr2][:, :]).then_inc(S["pool2"], 1)
            cast_w(0)
            if RC > 1:
                cast_w(1)
            for rc in range(RC):
                pr = rc % 2
                vector.wait_ge(S["pe2"], 3 * rc + 2)
                vector.wait_ge(S["act2"], rc + 1)
                if rc >= 1:
                    vector.wait_ge(S["dve2"], 2 * rc)
                vector.scalar_tensor_tensor(out=ug_sb[:, :], in0=psB[:, 0:T],
                                            scalar=bu_sb[:, rc:rc + 1], in1=g_sb[:, :],
                                            op0=ALU.add, op1=ALU.mult,
                                            accum_out=S_sb[:, rc:rc + 1]).then_inc(S["dve2"], 1)
                if rc >= 2:
                    vector.wait_ge(S["sp2" + str(pr)], 32 * (rc // 2))
                vector.wait_ge(S["dve2"], 2 * rc + 1)
                vector.tensor_tensor_scan(out=C_sb[pr][:, :], data0=ones_sb[:, :],
                                          data1=ug_sb[:, :], initial=0.0,
                                          op0=ALU.mult, op1=ALU.add).then_inc(S["dve2"], 1)
                vector.wait_ge(S["pe2"], 3 * rc + 3)
                for j in range(NLC):
                    vector.scalar_tensor_tensor(out=Vg_sb[pr][:, NL * j:NL * (j + 1)],
                                                in0=psA[:, NL * j:NL * (j + 1)],
                                                scalar=bv_sb[:, rc:rc + 1],
                                                in1=g_sb[:, NL * j:NL * (j + 1)],
                                                op0=ALU.add, op1=ALU.mult).then_inc(S["dve2v"], 1)
                if rc + 2 < RC:
                    cast_w(rc + 2)
                c = rc
                if c < DC:
                    if c >= 1:
                        vector.wait_ge(S["tsp0"], 16 * c)
                    vector.tensor_scalar(out=tfm0[:, :], in0=x1fm[c][:, 1:T + 1],
                                         scalar1=convw_sb[:, 3 * c:3 * c + 1],
                                         scalar2=convb_sb[:, c:c + 1],
                                         op0=ALU.mult, op1=ALU.add).then_inc(S["dve3"], 1)
                    vector.wait_ge(S["dve3"], 3 * c + 1)
                    vector.scalar_tensor_tensor(out=tfm0[:, :], in0=x1fm[c][:, 2:T + 2],
                                                scalar=convw_sb[:, 3 * c + 1:3 * c + 2],
                                                in1=tfm0[:, :], op0=ALU.mult, op1=ALU.add).then_inc(S["dve3"], 1)
                    vector.wait_ge(S["dve3"], 3 * c + 2)
                    vector.scalar_tensor_tensor(out=tfm0[:, :], in0=x1fm[c][:, 3:T + 3],
                                                scalar=convw_sb[:, 3 * c + 2:3 * c + 3],
                                                in1=tfm0[:, :], op0=ALU.mult, op1=ALU.add).then_inc(S["dve3"], 1)

    PE4 = NDC + DC
    VC = {}
    dv4 = [0]
    with nc.Block() as blk:
        @blk.gpsimd
        def _(gpsimd):
            def ld_t(lt):
                pr = lt % 2
                if lt >= 2:
                    gpsimd.wait_ge(S["pe4"], PE4 * (lt - 2) + PE4)
                if lt < 2:
                    gpsimd.wait_ge(S["tsp0"], 16 * DC)
                gpsimd.dma_start(
                    out=tin[pr][:, :].rearrange("p (dc l) -> p dc l", dc=DC),
                    in_=t_d[:, :, 128 * lt:128 * (lt + 1)].rearrange("dc p l -> p dc l"),
                ).then_inc(S["tld" + str(pr)], 16)

            ld_t(0)
            if LT > 1:
                ld_t(1)
            for ch in range(RC):
                prc = ch % 2
                if ch >= 2:
                    gpsimd.wait_ge(S["pool3"], ch - 1)
                gpsimd.dma_start(out=wun_stg[prc][:, :],
                                 in_=wuT[128 * ch:128 * (ch + 1), :]).then_inc(S["wuld" + str(prc)], 16)
            for lt in range(2, LT):
                ld_t(lt)
                if lt == min(5, LT - 1):
                    gpsimd.dma_start(
                        out=gstg[0][:, :].rearrange("p (c i) -> p c i", c=DC),
                        in_=gateT[:, 0:128].rearrange("(c p) i -> p c i", p=128),
                    ).then_inc(S["gld0"], 16)
                    gpsimd.dma_start(
                        out=ustg[0][:, :].rearrange("p (c i) -> p c i", c=DC),
                        in_=upT[:, 0:128].rearrange("(c p) i -> p c i", p=128),
                    ).then_inc(S["gld0"], 16)

        @blk.vector
        def _(vector):
            def wun_cast(ch):
                prc = ch % 2
                vector.wait_ge(S["wuld" + str(prc)], 16 * (ch // 2 + 1))
                vector.tensor_copy(out=wun_bf[ch][:, :], in_=wun_stg[prc][:, :]).then_inc(S["pool3"], 1)

            wun_cast(0)
            if RC > 1:
                wun_cast(1)
            vector.wait_ge(S["agld"], 16)
            for r in range(NCORES):
                if r >= 1:
                    vector.wait_ge(S["dve3b"], r)
                vector.scalar_tensor_tensor(out=O_sb[:, :], in0=ag_sb[:, RC * r:RC * (r + 1)],
                                            scalar=pm_sb[:, r:r + 1], in1=O_sb[:, :],
                                            op0=ALU.mult, op1=ALU.add).then_inc(S["dve3b"], 1)
            vector.wait_ge(S["dve3b"], NCORES)

            def inc(inst, tag):
                dv4[0] += 1
                VC[tag] = dv4[0]
                inst.then_inc(S["dve4"], 1)

            def emit_G(lt):
                p3 = lt % 3
                pr = lt % 2
                vector.wait_ge(S["ld4a" + str(p3)], 16 * (lt // 3 + 1))
                vector.wait_ge(S["ld4v" + str(p3)], 16 * (lt // 3 + 1))
                if lt >= 2:
                    vector.wait_ge(S["pe4"], PE4 * (lt - 2) + NDC)
                for rc in range(RC):
                    inc(vector.scalar_tensor_tensor(
                        out=G_sb[pr][:, 128 * rc:128 * (rc + 1)],
                        in0=Cin[p3][:, 128 * rc:128 * (rc + 1)],
                        scalar=O_sb[:, rc:rc + 1],
                        in1=Vgin[p3][:, 128 * rc:128 * (rc + 1)],
                        op0=ALU.add, op1=ALU.mult), ("gmc", lt, rc))
                VC[("gm", lt)] = VC[("gmc", lt, RC - 1)]

            emit_G(0)
            if LT > 1:
                emit_G(1)
            for ch in range(2, RC):
                wun_cast(ch)
            for lt in range(LT):
                pr = lt % 2
                vector.wait_ge(S["pe4"], PE4 * lt + PE4)
                vector.wait_ge(S["ld4b" + str(pr)], 16 * (lt // 2 + 1))
                if lt >= 1:
                    vector.wait_ge(S["sp4" + str((lt - 1) % 2)], 32 * ((lt - 1) // 2 + 1))
                inc(vector.scalar_tensor_tensor(out=xp_sb[:, :], in0=(psA if pr == 0 else psB)[:, 0:D],
                                                scalar=1.0, in1=xin4[pr][:, :], op0=ALU.mult,
                                                op1=ALU.add), ("xp", lt))
                if lt + 2 < LT:
                    emit_G(lt + 2)
                vector.wait_ge(S["dve4"], VC[("xp", lt)])
                inc(vector.reduce_sum(stat[:, 0:1], xp_sb[:, :], axis=AXX), ("red", lt))
                vector.wait_ge(S["act4"], 3 * lt + 3)
                inc(vector.reciprocal(stat[:, 4:5], stat[:, 3:4]), ("rec", lt))
                vector.wait_ge(S["dve4"], VC[("rec", lt)])
                inc(vector.tensor_mul(stat[:, 5:6], stat[:, 1:2], stat[:, 4:5]), ("nb", lt))
                if lt >= 2:
                    vector.wait_ge(S["sp4" + str(pr)], 32 * (lt // 2))
                vector.wait_ge(S["dve4"], VC[("nb", lt)])
                inc(vector.tensor_scalar(out=x2tm[pr][:, :], in0=xp_sb[:, :],
                                         scalar1=stat[:, 4:5], scalar2=stat[:, 5:6],
                                         op0=ALU.mult, op1=ALU.add), ("ap", lt))

        @blk.scalar
        def _(scalar):
            scalar.wait_ge(S["dve3"], 3 * DC)
            scalar.dma_start(out=t_d[DC - 1, :, :],
                             in_=tfm0[:, :]).then_inc(S["tsp0"], 16)

            def ld_v(lt3):
                p3 = lt3 % 3
                if lt3 >= 3:
                    scalar.wait_ge(S["dve4"], VC[("gm", lt3 - 3)])
                scalar.dma_start(
                    out=Vgin[p3][:, :].rearrange("p (rc l) -> p rc l", rc=RC),
                    in_=Vg_d[:, :, 128 * lt3:128 * (lt3 + 1)].rearrange("rc p l -> p rc l"),
                ).then_inc(S["ld4v" + str(p3)], 16)

            def ld_b(lt2):
                pr2 = lt2 % 2
                if lt2 >= 2:
                    scalar.wait_ge(S["act4"], 3 * (lt2 - 2) + 2)
                scalar.dma_start(out=xin4[pr2][:, :],
                                 in_=x_pad[2 + 128 * lt2:2 + 128 * (lt2 + 1), :]).then_inc(S["ld4b" + str(pr2)], 16)

            ld_v(0)
            ld_b(0)
            if LT > 1:
                ld_v(1)
                ld_b(1)
            if LT > 2:
                ld_v(2)
            scalar.wait_ge(S["cc"], 1)
            scalar.dma_start(
                out=ag_sb[:, 0:NCORES * RC].rearrange("p (r c) -> p r c", r=NCORES),
                in_=cc_out[:, :].rearrange("r (p c) -> p r c", p=128),
            ).then_inc(S["agld"], 16)
            for lt in range(LT):
                pr = lt % 2
                if lt == min(NQ // 128, LT - 1):
                    nq = NQ // 128
                    scalar.wait_ge(S["sp40"], 32 * ((nq + 1) // 2))
                    scalar.wait_ge(S["sp41"], 32 * (nq // 2))
                    for c in range(DC):
                        scalar.dma_start(out=x2q0[c][:, :],
                                         in_=x2tm_d[0:NQ, 128 * c:128 * (c + 1)],
                                         transpose=True).then_inc(S["xbq0"], 16)
                scalar.wait_ge(S["dve4"], VC[("red", lt)])
                if lt >= 1:
                    scalar.wait_ge(S["act4"], 3 * lt)
                scalar.mul(stat[:, 1:2], stat[:, 0:1], -invD).then_inc(S["act4"], 1)
                scalar.wait_ge(S["act4"], 3 * lt + 1)
                scalar.activation(out=xin4[pr][:, :], in_=xp_sb[:, :], func=AF.Square,
                                  bias=stat[:, 1:2], accum_out=stat[:, 2:3]).then_inc(S["act4"], 1)
                scalar.wait_ge(S["act4"], 3 * lt + 2)
                scalar.activation(out=stat[:, 3:4], in_=stat[:, 2:3], func=AF.Sqrt,
                                  scale=cvals[:, 1:2], bias=cvals[:, 0:1]).then_inc(S["act4"], 1)
                if lt + 2 < LT:
                    ld_b(lt + 2)
                if lt + 3 < LT:
                    ld_v(lt + 3)
                scalar.wait_ge(S["dve4"], VC[("ap", lt)])
                scalar.dma_start(out=xp_d[128 * lt:128 * (lt + 1), :],
                                 in_=xp_sb[:, :]).then_inc(S["sp4" + str(pr)], 16)
                scalar.dma_start(out=x2tm_d[128 * lt:128 * (lt + 1), :],
                                 in_=x2tm[pr][:, :]).then_inc(S["sp4" + str(pr)], 16)
            scalar.wait_ge(S["sp40"], 32 * ((LT + 1) // 2))
            scalar.wait_ge(S["sp41"], 32 * (LT // 2))

        @blk.sync
        def _(sync):
            def ld_a(lt):
                p3 = lt % 3
                if lt >= 3:
                    sync.wait_ge(S["dve4"], VC[("gm", lt - 3)])
                sync.dma_start(
                    out=Cin[p3][:, :].rearrange("p (rc l) -> p rc l", rc=RC),
                    in_=C_d[:, :, 128 * lt:128 * (lt + 1)].rearrange("rc p l -> p rc l"),
                ).then_inc(S["ld4a" + str(p3)], 16)

            for lt in range(LT):
                ld_a(lt)


        @blk.tensor
        def _(tensor):
            for lt in range(LT):
                pr = lt % 2
                ps = psA if pr == 0 else psB
                if lt >= 2:
                    tensor.wait_ge(S["dve4"], VC[("xp", lt - 2)])
                for rc in range(RC):
                    tensor.wait_ge(S["dve4"], VC[("gmc", lt, rc)])
                    if lt == 0:
                        tensor.wait_ge(S["pool3"], rc + 1)
                    for n in range(NDC):
                        inst = tensor.matmul(ps[:, ND * n:ND * (n + 1)],
                                             G_sb[pr][:, 128 * rc:128 * (rc + 1)],
                                             wun_bf[rc][:, ND * n:ND * (n + 1)],
                                             start=(rc == 0), stop=False)
                        if rc == RC - 1:
                            inst.then_inc(S["pe4"], 1)
                tensor.wait_ge(S["tld" + str(pr)], 16 * (lt // 2 + 1))
                for dc in range(DC):
                    tensor.matmul(ps[:, 128 * dc:128 * (dc + 1)],
                                  tin[pr][:, 128 * dc:128 * (dc + 1)],
                                  ident[:, :], is_transpose=True,
                                  start=False, stop=((dc + 1) % 4 == 0 or dc == DC - 1)
                                  ).then_inc(S["pe4"], 1)

        @blk.sync
        def _(sync):
            sync.wait_ge(S["pe4"], PE4 * LT)
            sync.wait_ge(S["sp40"], 32 * ((LT + 1) // 2))
            sync.wait_ge(S["sp41"], 32 * (LT // 2))
            sync.dma_start(
                out=gstg[1][:, :].rearrange("p (c i) -> p c i", c=DC),
                in_=gateT[:, 128:256].rearrange("(c p) i -> p c i", p=128),
            ).then_inc(S["gld1"], 16)
            sync.dma_start(
                out=ustg[1][:, :].rearrange("p (c i) -> p c i", c=DC),
                in_=upT[:, 128:256].rearrange("(c p) i -> p c i", p=128),
            ).then_inc(S["gld1"], 16)
            for q in range(1, NQC):
                sem = [None, S["xbq1"], S["xb123"], S["xbq3"]][q]
                for c in range(DC):
                    sync.dma_start(out=x2fm[c][:, NQ * q:NQ * (q + 1)],
                                   in_=x2tm_d[NQ * q:NQ * (q + 1), 128 * c:128 * (c + 1)],
                                   transpose=True).then_inc(sem, 16)

        @blk.sync
        def _(sync):
            for ic in range(2, IC):
                pr = ic % 2
                sync.wait_ge(S["pool5"], 2 * (ic - 2) + 2)
                sync.dma_start(
                    out=gstg[pr][:, :].rearrange("p (c i) -> p c i", c=DC),
                    in_=gateT[:, 128 * ic:128 * (ic + 1)].rearrange("(c p) i -> p c i", p=128),
                ).then_inc(S["gld" + str(pr)], 16)
                sync.dma_start(
                    out=ustg[pr][:, :].rearrange("p (c i) -> p c i", c=DC),
                    in_=upT[:, 128 * ic:128 * (ic + 1)].rearrange("(c p) i -> p c i", p=128),
                ).then_inc(S["gld" + str(pr)], 16)

        @blk.tensor
        def _(tensor):
            def x2src(k, n):
                if n == 0:
                    return x2q0[k][:, :]
                return x2fm[k][:, NL * n:NL * (n + 1)]

            for ic in range(IC):
                pr = ic % 2
                tensor.wait_ge(S["pool5"], 2 * ic + 2)
                if ic >= 1:
                    tensor.wait_ge(S["act5"], ic)
                for n in range(NLC):
                    if ic == 0:
                        sem = [S["xbq0"], S["xbq1"], S["xb123"], S["xbq3"]][n]
                        tensor.wait_ge(sem, 16 * DC)
                    for k in range(DC):
                        inst = tensor.matmul(psA[:, NL * n:NL * (n + 1)],
                                             gbf[pr][:, 128 * k:128 * (k + 1)],
                                             x2src(k, n),
                                             start=(k == 0), stop=(k == DC - 1))
                        if n == NLC - 1 and k == DC - 1:
                            inst.then_inc(S["pe5"], 1)
                if ic >= 1:
                    tensor.wait_ge(S["dve5"], ic)
                for n in range(NLC):
                    for k in range(DC):
                        inst = tensor.matmul(psB[:, NL * n:NL * (n + 1)],
                                             ubf[pr][:, 128 * k:128 * (k + 1)],
                                             x2src(k, n),
                                             start=(k == 0), stop=(k == DC - 1))
                        if n == NLC - 1 and k == DC - 1:
                            inst.then_inc(S["pe5"], 1)

        @blk.scalar
        def _(scalar):
            for ic in range(IC):
                pr = ic % 2
                scalar.wait_ge(S["pe5"], 2 * ic + 1)
                if ic >= 1:
                    scalar.wait_ge(S["dve5"], ic)
                scalar.activation(out=silu_sb[:, :], in_=psA[:, 0:T], func=AF.Silu,
                                  scale=gs_sb[:, ic:ic + 1]).then_inc(S["act5"], 1)
                if ic >= 1:
                    scalar.dma_start(out=hf_d[ic - 1, :, :],
                                     in_=hf_sb[(ic - 1) % 2][:, :]).then_inc(S["hsp" + str((ic - 1) % 2)], 16)
            scalar.wait_ge(S["dve5"], IC)
            scalar.dma_start(out=hf_d[IC - 1, :, :],
                             in_=hf_sb[(IC - 1) % 2][:, :]).then_inc(S["hsp" + str((IC - 1) % 2)], 16)
            scalar.wait_ge(S["hsp0"], 16 * ((IC + 1) // 2))
            scalar.wait_ge(S["hsp1"], 16 * (IC // 2))

        @blk.vector
        def _(vector):
            def deq(ic2):
                pr2 = ic2 % 2
                vector.wait_ge(S["gld" + str(pr2)], 32 * (ic2 // 2 + 1))
                if ic2 >= 2:
                    vector.wait_ge(S["pe5"], 2 * (ic2 - 2) + 2)
                vector.tensor_scalar(out=gbf[pr2][:, :], in0=gstg[pr2][:, :], scalar1=-128.0,
                                     scalar2=None, op0=ALU.add).then_inc(S["pool5"], 1)
                vector.tensor_scalar(out=ubf[pr2][:, :], in0=ustg[pr2][:, :], scalar1=-128.0,
                                     scalar2=None, op0=ALU.add).then_inc(S["pool5"], 1)
            vector.wait_ge(S["sp40"], 32 * ((LT + 1) // 2))
            vector.wait_ge(S["sp41"], 32 * (LT // 2))
            deq(0)
            if IC > 1:
                deq(1)
            for ic in range(IC):
                pr = ic % 2
                if ic + 2 < IC:
                    deq(ic + 2)
                vector.wait_ge(S["pe5"], 2 * ic + 2)
                vector.wait_ge(S["act5"], ic + 1)
                if ic >= 2:
                    vector.wait_ge(S["hsp" + str(pr)], 16 * (ic // 2))
                vector.scalar_tensor_tensor(out=hf_sb[pr][:, :], in0=psB[:, 0:T],
                                            scalar=us_sb[:, ic:ic + 1], in1=silu_sb[:, :],
                                            op0=ALU.mult, op1=ALU.mult).then_inc(S["dve5"], 1)

        @blk.sync
        def _(sync):
            hg = [S["hga"], S["hgb"]]
            sync.wait_ge(S["hsp0"], 16 * ((IC + 1) // 2))
            sync.wait_ge(S["hsp1"], 16 * (IC // 2))
            for g in range(IC // 8):
                if g >= 2:
                    sync.wait_ge(hg[g % 2], 128 * (g // 2))
                for j in range(8):
                    ic = 8 * g + j
                    sync.dma_start(out=hfh[ic][:, :],
                                   in_=hf_d[ic, :, 0:TH]).then_inc(hg[g % 2], 16)

        @blk.gpsimd
        def _(gpsimd):
            gpsimd.wait_ge(S["pe4"], PE4 * LT)
            for lh in range(LH):
                for dc in range(NDC):
                    for ic in range(IC):
                        seq = (lh * NDC + dc) * IC + ic
                        pr = seq % 4
                        if seq >= 4:
                            gpsimd.wait_ge(S["pool6"], seq - 3)
                        gpsimd.dma_start(out=dstg[pr][:, :],
                                         in_=downT[128 * ic:128 * (ic + 1),
                                                   ND * dc:ND * (dc + 1)]).then_inc(S["dld" + str(pr)], 16)

        @blk.tensor
        def _(tensor):
            hg = [S["hga"], S["hgb"]]
            for lh in range(LH):
                for dc in range(NDC):
                    dcseq = lh * NDC + dc
                    for ic in range(IC):
                        if dc == 0 and ic % 8 == 0:
                            g = lh * (IC // 8) + ic // 8
                            tensor.wait_ge(hg[g % 2], 128 * (g // 2 + 1))
                        seq = dcseq * IC + ic
                        pr = seq % 4
                        tensor.wait_ge(S["pool6"], seq + 1)
                        last = None
                        for lt in range(LTH):
                            if ic == 0 and dcseq >= 1:
                                tensor.wait_ge(S["dve6"], (dcseq - 1) * 2 * LTH + 2 * (lt + 1))
                            ps = psall[lt // 4] if LTH > 4 else psall[0]
                            slc = ps[:, 512 * (lt % 4):512 * (lt % 4) + ND]
                            last = tensor.matmul(slc,
                                                 hfh[ic][:, 128 * lt:128 * (lt + 1)],
                                                 dbf[pr][:, :],
                                                 start=(ic == 0), stop=(ic == IC - 1))
                        last.then_inc(S["pe6"], 1)

        @blk.scalar
        def _(scalar):
            for lh in range(LH):
                for dc in range(NDC):
                    dcseq = lh * NDC + dc
                    pr = dcseq % 2
                    if dcseq >= 2:
                        scalar.wait_ge(S["dve6"], (dcseq - 1) * 2 * LTH)
                    scalar.dma_start(
                        out=xr_sb[pr][:, :].rearrange("p (lt d) -> p lt d", lt=LTH),
                        in_=xp_d[TH * lh:TH * (lh + 1), ND * dc:ND * (dc + 1)].rearrange("(lt p) d -> p lt d", p=128),
                    ).then_inc(S["xrld" if pr == 0 else "xrld1"], 16)
                    if lh == 0 and dc == NDC - 1:
                        hg = [S["hga"], S["hgb"]]
                        for ic in range(IC):
                            if ic % 8 == 0:
                                g = (IC // 8) + ic // 8
                                scalar.wait_ge(hg[g % 2], 128 * (g // 2))
                            scalar.wait_ge(S["pe6"], (NDC - 1) * IC + ic + 1)
                            scalar.dma_start(out=hfh[ic][:, :],
                                             in_=hf_d[ic, :, TH:2 * TH]).then_inc(hg[g % 2], 16)
                    for lt in range(LTH):
                        oseq = dcseq * LTH + lt
                        scalar.wait_ge(S["dve6"], dcseq * 2 * LTH + 2 * (lt + 1))
                        scalar.dma_start(
                            out=out_d[TH * lh + 128 * lt:TH * lh + 128 * (lt + 1),
                                      ND * dc:ND * (dc + 1)],
                            in_=outb[oseq % 4][:, :],
                        ).then_inc(S["osp" + str(oseq % 4)], 16)

        @blk.vector
        def _(vector):
            def deq6(seq):
                pr4 = seq % 4
                vector.wait_ge(S["dld" + str(pr4)], 16 * (seq // 4 + 1))
                if seq >= 4:
                    vector.wait_ge(S["pe6"], seq - 3)
                vector.tensor_scalar(out=dbf[pr4][:, :], in0=dstg[pr4][:, :], scalar1=-128.0,
                                     scalar2=None, op0=ALU.add).then_inc(S["pool6"], 1)

            def evacs(dcseq):
                dc = dcseq % NDC
                pr = dcseq % 2
                vector.wait_ge(S["xrld" if pr == 0 else "xrld1"], 16 * (dcseq // 2 + 1))
                vector.wait_ge(S["pe6"], (dcseq + 1) * IC)
                for lt in range(LTH):
                    oseq = dcseq * LTH + lt
                    ps = psall[lt // 4] if LTH > 4 else psall[0]
                    slc = ps[:, 512 * (lt % 4):512 * (lt % 4) + ND]
                    ob = outb[oseq % 4]
                    if oseq >= 4:
                        vector.wait_ge(S["osp" + str(oseq % 4)], 16 * (oseq // 4))
                    vector.tensor_mul(out=ob[:, :], in0=slc,
                                      in1=sd_sb[:, ND * dc:ND * (dc + 1)]).then_inc(S["dve6"], 1)
                    vector.wait_ge(S["dve6"], dcseq * 2 * LTH + 2 * lt + 1)
                    vector.tensor_add(out=ob[:, :], in0=ob[:, :],
                                      in1=xr_sb[pr][:, lt * ND:(lt + 1) * ND]).then_inc(S["dve6"], 1)

            NSEQ = LH * NDC
            for dcseq in range(NSEQ):
                for ic in range(4):
                    deq6(dcseq * IC + ic)
                if dcseq >= 1:
                    evacs(dcseq - 1)
                for ic in range(4, IC):
                    deq6(dcseq * IC + ic)
            evacs(NSEQ - 1)

        @blk.sync
        def _(sync):
            total = LH * NDC * LTH
            for j in range(4):
                sync.wait_ge(S["osp" + str(j)], 16 * ((total - j + 3) // 4))

    return nc


def _prep_inputs(L0_attn_x, L0_attn_Wu, L0_attn_bu, L0_attn_Wv, L0_attn_bv,
                 L0_attn_Wg1, L0_attn_bg1, L0_attn_Wg2, L0_attn_bg2,
                 L0_attn_conv_w, L0_attn_conv_b, L0_ffn_gate_w, L0_ffn_gate_scale,
                 L0_ffn_up_w, L0_ffn_up_scale, L0_ffn_down_w, L0_ffn_down_scale,
                 cfg=FULL, ncores=NCORES):
    T, D, R, DH, I = cfg["T"], cfg["D"], cfg["R"], cfg["DH"], cfg["I"]
    TP = T + 128
    RCn, DHCn, DCn, ICn = R // 128, DH // 128, D // 128, I // 128
    x = np.asarray(L0_attn_x, dtype=np.float32).reshape(-1, D)
    Ltot = x.shape[0]
    f32c = lambda a: np.ascontiguousarray(np.asarray(a, dtype=np.float32))
    u8c = lambda a: np.ascontiguousarray(np.asarray(a).astype(np.uint8))

    shared = dict(
        wuT=f32c(np.asarray(L0_attn_Wu).T),
        wvT=f32c(np.asarray(L0_attn_Wv).T),
        wg1T=f32c(np.asarray(L0_attn_Wg1).T),
        wg2T=f32c(np.asarray(L0_attn_Wg2).T),
        bu_r=f32c(np.asarray(L0_attn_bu).reshape(RCn, 128).T),
        bv_r=f32c(np.asarray(L0_attn_bv).reshape(RCn, 128).T),
        bg1_r=f32c(np.asarray(L0_attn_bg1).reshape(DHCn, 128).T),
        bg2_r=f32c(np.asarray(L0_attn_bg2).reshape(RCn, 128).T),
        convw_r=f32c(np.asarray(L0_attn_conv_w).reshape(D, 3).reshape(DCn, 128, 3)
                     .transpose(1, 0, 2).reshape(128, 3 * DCn)),
        convb_r=f32c(np.asarray(L0_attn_conv_b).reshape(DCn, 128).T),
        gateT=u8c(np.asarray(L0_ffn_gate_w).T),
        upT=u8c(np.asarray(L0_ffn_up_w).T),
        downT=u8c(np.asarray(L0_ffn_down_w).T),
        gs_r=f32c(np.asarray(L0_ffn_gate_scale).reshape(ICn, 128).T),
        us_r=f32c(np.asarray(L0_ffn_up_scale).reshape(ICn, 128).T),
        sd_b=f32c(np.broadcast_to(np.asarray(L0_ffn_down_scale, dtype=np.float32), (128, D))),
        ident_in=np.eye(128, dtype=np.float32),
    )
    in_maps = []
    Tcore = Ltot // ncores
    assert Tcore == T, (Tcore, T)
    for core in range(ncores):
        s = core * T
        xp = np.zeros((TP, D), dtype=np.float32)
        lo = max(s - 1, 0)
        hi = min(s + T + 1, Ltot)
        xp[2 + (lo - s):2 + (hi - s), :] = x[lo:hi]
        pm = np.zeros((128, ncores), dtype=np.float32)
        pm[:, :core] = 1.0
        m = dict(shared)
        m["x_pad"] = xp
        m["premask"] = pm
        in_maps.append(m)
    return in_maps


def kernel(**inputs):
    cfg = FULL
    in_maps = _prep_inputs(cfg=cfg, ncores=NCORES, **inputs)
    nc = build_nc(**cfg)
    res = run_bass_kernel_spmd(nc, in_maps, core_ids=list(range(NCORES)))
    out = np.concatenate([r["out_d"] for r in res.results], axis=0)
    return out.reshape(1, NCORES * cfg["T"], cfg["D"]).astype(np.float32)


# version 30
# speedup vs baseline: 1.0035x; 1.0035x over previous
import numpy as np
import concourse.bass as bass
import concourse.mybir as mybir
from concourse.bass_utils import run_bass_kernel_spmd

F32 = mybir.dt.float32
BF16 = mybir.dt.bfloat16
I32 = mybir.dt.int32
U8 = mybir.dt.uint8
AF = mybir.ActivationFunctionType
ALU = mybir.AluOpType
AXX = mybir.AxisListType.X

NCORES = 8
EPS = 1e-5

FULL = dict(T=2048, D=2048, R=2048, DH=512, I=8192)


def build_nc(T=2048, D=2048, R=2048, DH=512, I=8192):
    TP = T + 128
    LT = T // 128
    DC = D // 128
    RC = R // 128
    DHC = DH // 128
    IC = I // 128
    NL = min(512, T); NLC = T // NL
    NQC = NLC
    ND = min(512, D); NDC = D // ND
    LH = 2
    TH = T // LH
    LTH = TH // 128
    invD = 1.0 / D

    nc = bass.Bass("TRN2", target_bir_lowering=False)

    x_pad = nc.dram_tensor("x_pad", [TP, D], F32, kind="ExternalInput").ap()
    wuT = nc.dram_tensor("wuT", [D, R], F32, kind="ExternalInput").ap()
    wvT = nc.dram_tensor("wvT", [D, R], F32, kind="ExternalInput").ap()
    wg1T = nc.dram_tensor("wg1T", [D, DH], F32, kind="ExternalInput").ap()
    wg2T = nc.dram_tensor("wg2T", [DH, R], F32, kind="ExternalInput").ap()
    bu_r = nc.dram_tensor("bu_r", [128, RC], F32, kind="ExternalInput").ap()
    bv_r = nc.dram_tensor("bv_r", [128, RC], F32, kind="ExternalInput").ap()
    bg1_r = nc.dram_tensor("bg1_r", [128, DHC], F32, kind="ExternalInput").ap()
    bg2_r = nc.dram_tensor("bg2_r", [128, RC], F32, kind="ExternalInput").ap()
    convw_r = nc.dram_tensor("convw_r", [128, 3 * DC], F32, kind="ExternalInput").ap()
    convb_r = nc.dram_tensor("convb_r", [128, DC], F32, kind="ExternalInput").ap()
    gateT = nc.dram_tensor("gateT", [D, I], U8, kind="ExternalInput").ap()
    upT = nc.dram_tensor("upT", [D, I], U8, kind="ExternalInput").ap()
    downT = nc.dram_tensor("downT", [I, D], U8, kind="ExternalInput").ap()
    gs_r = nc.dram_tensor("gs_r", [128, IC], F32, kind="ExternalInput").ap()
    us_r = nc.dram_tensor("us_r", [128, IC], F32, kind="ExternalInput").ap()
    sd_b = nc.dram_tensor("sd_b", [128, D], F32, kind="ExternalInput").ap()
    premask = nc.dram_tensor("premask", [128, NCORES], F32, kind="ExternalInput").ap()
    ident_in = nc.dram_tensor("ident_in", [128, 128], F32, kind="ExternalInput").ap()

    out_d = nc.dram_tensor("out_d", [T, D], F32, kind="ExternalOutput").ap()

    x1tm_d = nc.dram_tensor("x1tm_d", [TP, D], BF16, kind="Internal").ap()
    x2tm_d = nc.dram_tensor("x2tm_d", [T, D], BF16, kind="Internal").ap()
    C_d = nc.dram_tensor("C_d", [RC, 128, T], BF16, kind="Internal").ap()
    Vg_d = nc.dram_tensor("Vg_d", [RC, 128, T], BF16, kind="Internal").ap()
    t_d = nc.dram_tensor("t_d", [DC, 128, T], F32, kind="Internal").ap()
    xp_d = nc.dram_tensor("xp_d", [T, D], F32, kind="Internal").ap()
    hf_d = nc.dram_tensor("hf_d", [IC, 128, T], BF16, kind="Internal").ap()
    cc_in = nc.dram_tensor("cc_in", [1, R], F32, kind="Internal").ap()
    cc_out = nc.dram_tensor("cc_out", [NCORES, R], F32, kind="Internal").ap()

    f32b = lambda n: (4 * n + 31) // 32 * 32
    bfb = lambda n: (2 * n + 31) // 32 * 32
    BASE = (nc._sbuf_addr_for_side("left") + 31) // 32 * 32
    TOPA = nc._sbuf_addr_for_side("right")
    LIMIT = TOPA - BASE - 3584
    nc.alloc_sbuf_tensor("manual_guard", [128, LIMIT], mybir.dt.uint8)
    _dtb = {F32: 4, BF16: 2, I32: 4, U8: 1}

    def at(name, shape, dtype, offset):
        offset = BASE + (offset + 31) // 32 * 32
        assert offset + shape[-1] * _dtb[dtype] <= BASE + LIMIT, name
        return nc.alloc_sbuf_tensor_at(name, shape, dtype, offset=offset)

    p = 0
    bu_sb = at("bu_sb", [128, RC], F32, offset=p); p += f32b(RC)
    bv_sb = at("bv_sb", [128, RC], F32, offset=p); p += f32b(RC)
    bg1_sb = at("bg1_sb", [128, DHC], F32, offset=p); p += f32b(DHC)
    bg2_sb = at("bg2_sb", [128, RC], F32, offset=p); p += f32b(RC)
    convw_sb = at("convw_sb", [128, 3 * DC], F32, offset=p); p += f32b(3 * DC)
    convb_sb = at("convb_sb", [128, DC], F32, offset=p); p += f32b(DC)
    gs_sb = at("gs_sb", [128, IC], F32, offset=p); p += f32b(IC)
    us_sb = at("us_sb", [128, IC], F32, offset=p); p += f32b(IC)
    sd_sb = at("sd_sb", [128, D], F32, offset=p); p += f32b(D)
    pm_sb = at("pm_sb", [128, NCORES], F32, offset=p); p += f32b(NCORES)
    ident = at("ident", [128, 128], F32, offset=p); p += f32b(128)
    ones_sb = at("ones_sb", [128, T], F32, offset=p); p += f32b(T)
    S_sb = at("S_sb", [128, RC], F32, offset=p); p += f32b(RC)
    O_sb = at("O_sb", [128, RC], F32, offset=p); p += f32b(RC)
    ag_sb = at("ag_sb", [128, NCORES * RC], F32, offset=p); p += f32b(NCORES * RC)
    stat = at("stat", [128, 12], F32, offset=p); p += f32b(12)
    cvals = at("cvals", [128, 2], F32, offset=p); p += f32b(2)
    PERM_END = (p + 63) // 64 * 64
    assert PERM_END <= 24 * 1024, PERM_END

    X1W = TP
    x1fm = [at(f"x1fm{c}", [128, X1W], BF16, offset=PERM_END + c * bfb(X1W)) for c in range(DC)]
    X1_END = PERM_END + DC * bfb(X1W)
    hfm = [at(f"hfm{c}", [128, T], BF16, offset=X1_END + c * bfb(T)) for c in range(DHC)]
    H_END = X1_END + DHC * bfb(T)

    p = H_END
    xin = [at(f"xin{j}", [128, D], F32, offset=p + j * f32b(D)) for j in range(2)]; p += 2 * f32b(D)
    sq_sb = at("sq_sb", [128, D], F32, offset=p); p += f32b(D)
    x1tm = [at(f"x1tm{j}", [128, D], BF16, offset=p + j * bfb(D)) for j in range(2)]; p += 2 * bfb(D)
    assert p <= LIMIT, p

    p = H_END
    wstg = [at(f"wstg{j}", [128, 2 * D], F32, offset=p + j * f32b(2 * D)) for j in range(2)]; p += 2 * f32b(2 * D)
    wubf = [at(f"wubf{j}", [128, D], BF16, offset=p + j * bfb(D)) for j in range(2)]; p += 2 * bfb(D)
    wvbf = [at(f"wvbf{j}", [128, D], BF16, offset=p + j * bfb(D)) for j in range(2)]; p += 2 * bfb(D)
    wg2stg = [at(f"wg2stg{j}", [128, DH], F32, offset=p + j * f32b(DH)) for j in range(2)]; p += 2 * f32b(DH)
    wg2bf = [at(f"wg2bf{j}", [128, DH], BF16, offset=p + j * bfb(DH)) for j in range(2)]; p += 2 * bfb(DH)
    g_sb = at("g_sb", [128, T], F32, offset=p); p += f32b(T)
    ug_sb = at("ug_sb", [128, T], F32, offset=p); p += f32b(T)
    wg1bf = [at(f"wg1bf{c}", [128, D], BF16, offset=p + c * bfb(D)) for c in range(DHC)]
    C_sb = [at(f"C_sb{j}", [128, T], BF16, offset=p + j * bfb(T)) for j in range(2)]; p += 2 * bfb(T)
    Vg_sb = [at(f"Vg_sb{j}", [128, T], BF16, offset=p + j * bfb(T)) for j in range(2)]; p += 2 * bfb(T)
    tfm0 = at("tfm0", [128, T], F32, offset=p)
    TFM_END = p + f32b(T)
    assert TFM_END <= LIMIT, p
    assert DHC * bfb(D) <= 2 * f32b(T) + 2 * bfb(T)

    p = H_END
    wun_stg = [at(f"wun_stg{j}", [128, D], F32, offset=p + j * f32b(D)) for j in range(2)]; p += 2 * f32b(D)
    WUN_START = p
    wun_bf = [at(f"wun_bf{c}", [128, D], BF16, offset=p + c * bfb(D)) for c in range(RC)]; p += RC * bfb(D)
    assert p <= LIMIT, p
    PH3_END = p

    NQ = min(512, T)
    p = PH3_END
    x2q0 = [at(f"x2q0{c}", [128, NQ], BF16, offset=p + c * bfb(NQ)) for c in range(DC)]; p += DC * bfb(NQ)
    gstg0_e = at("gstg0_e", [128, D], U8, offset=p); p += (D + 31) // 32 * 32
    ustg0_e = at("ustg0_e", [128, D], U8, offset=p); p += (D + 31) // 32 * 32
    assert p <= LIMIT, p

    sz4 = (3 * bfb(R) + 3 * bfb(R) + 2 * f32b(D) + 2 * f32b(D) + 2 * bfb(R)
           + f32b(D) + 2 * bfb(D))
    p = PERM_END if PERM_END + sz4 <= WUN_START else max(PH3_END, TFM_END)
    Cin = [at(f"Cin{j}", [128, R], BF16, offset=p + j * bfb(R)) for j in range(3)]; p += 3 * bfb(R)
    Vgin = [at(f"Vgin{j}", [128, R], BF16, offset=p + j * bfb(R)) for j in range(3)]; p += 3 * bfb(R)
    tin = [at(f"tin{j}", [128, D], F32, offset=p + j * f32b(D)) for j in range(2)]; p += 2 * f32b(D)
    xin4 = [at(f"xin4{j}", [128, D], F32, offset=p + j * f32b(D)) for j in range(2)]; p += 2 * f32b(D)
    G_sb = [at(f"G_sb{j}", [128, R], BF16, offset=p + j * bfb(R)) for j in range(2)]; p += 2 * bfb(R)
    xp_sb = at("xp_sb", [128, D], F32, offset=p); p += f32b(D)
    x2tm = [at(f"x2tm{j}", [128, D], BF16, offset=p + j * bfb(D)) for j in range(2)]; p += 2 * bfb(D)
    assert p <= LIMIT and (p <= WUN_START or p > PH3_END), (p, WUN_START, PH3_END)

    p = PERM_END
    u8b = lambda n: (n + 31) // 32 * 32
    x2fm = [at(f"x2fm{c}", [128, T], BF16, offset=p + c * bfb(T)) for c in range(DC)]; p += DC * bfb(T)
    gstg = [gstg0_e, at("gstg1", [128, D], U8, offset=p)]; p += u8b(D)
    ustg = [ustg0_e, at("ustg1", [128, D], U8, offset=p)]; p += u8b(D)
    gbf = [at(f"gbf{j}", [128, D], BF16, offset=p + j * bfb(D)) for j in range(2)]; p += 2 * bfb(D)
    ubf = [at(f"ubf{j}", [128, D], BF16, offset=p + j * bfb(D)) for j in range(2)]; p += 2 * bfb(D)
    silu_sb = at("silu_sb", [128, T], F32, offset=p); p += f32b(T)
    hf_sb = [at(f"hf_sb{j}", [128, T], BF16, offset=p + j * bfb(T)) for j in range(2)]; p += 2 * bfb(T)
    assert p <= LIMIT, p

    p = PERM_END
    hfh = [at(f"hfh{c}", [128, TH], BF16, offset=p + c * bfb(TH)) for c in range(IC)]; p += IC * bfb(TH)
    dstg = [at(f"dstg{j}", [128, ND], U8, offset=p + j * u8b(ND)) for j in range(4)]; p += 4 * u8b(ND)
    dbf = [at(f"dbf{j}", [128, ND], BF16, offset=p + j * bfb(ND)) for j in range(4)]; p += 4 * bfb(ND)
    xr_sb = [at(f"xr_sb{j}", [128, LTH * ND], F32, offset=p + j * f32b(LTH * ND)) for j in range(2)]; p += 2 * f32b(LTH * ND)
    outb = [at(f"outb{j}", [128, ND], F32, offset=p + j * f32b(ND)) for j in range(4)]; p += 4 * f32b(ND)
    assert p <= LIMIT, p

    psA = nc.alloc_psum_tensor("psA", [128, 2048], F32)
    psB = nc.alloc_psum_tensor("psB", [128, 2048], F32)
    psall = [psA, psB]

    S = {}
    for n in ("pl0", "xld0", "xld1", "dve1", "act1", "x1sp0", "x1sp1",
              "xb1", "wg1ld", "wg1ld1", "pool_h", "pe_h", "act_h",
              "wld0", "wld1", "pool2", "pe2", "act2", "dve2", "sp20", "sp21",
              "wuld0", "wuld1", "pool3", "act3", "dve3", "tsp0", "tsp1",
              "ccsp", "cc", "agld", "dve3b",
              "ld4a0", "ld4a1", "ld4a2", "ld4v0", "ld4v1", "ld4v2",
              "tld0", "tld1", "ld4b0", "ld4b1", "dve4", "act4", "pe4", "sp40", "sp41",
              "xbq0", "xbq1", "xbq3", "xb123", "dve2v",
              "gld0", "gld0e", "gld1", "pool5", "pe5", "act5", "dve5", "hsp0", "hsp1",
              "hga", "hgb", "dld0", "dld1", "dld2", "dld3", "pool6", "pe6", "dve6", "xrld", "xrld1", "osp0", "osp1", "osp2", "osp3"):
        S[n] = nc.alloc_semaphore(n)
    for c in range(D // 128):
        S[f"xbs{c}"] = nc.alloc_semaphore(f"xbs{c}")

    with nc.Block() as blk:
        @blk.sync
        def _(sync):
            sync.dma_start(out=bu_sb[:, :], in_=bu_r[:, :]).then_inc(S["pl0"], 16)
            sync.dma_start(out=bv_sb[:, :], in_=bv_r[:, :]).then_inc(S["pl0"], 16)
            sync.dma_start(out=bg1_sb[:, :], in_=bg1_r[:, :]).then_inc(S["pl0"], 16)
            sync.dma_start(out=bg2_sb[:, :], in_=bg2_r[:, :]).then_inc(S["pl0"], 16)
            sync.dma_start(out=convw_sb[:, :], in_=convw_r[:, :]).then_inc(S["pl0"], 16)
            sync.dma_start(out=convb_sb[:, :], in_=convb_r[:, :]).then_inc(S["pl0"], 16)
            sync.dma_start(out=gs_sb[:, :], in_=gs_r[:, :]).then_inc(S["pl0"], 16)
            sync.dma_start(out=us_sb[:, :], in_=us_r[:, :]).then_inc(S["pl0"], 16)
            sync.dma_start(out=sd_sb[:, :], in_=sd_b[:, :]).then_inc(S["pl0"], 16)
            sync.dma_start(out=pm_sb[:, :], in_=premask[:, :]).then_inc(S["pl0"], 16)
            sync.dma_start(out=ident[:, :], in_=ident_in[:, :]).then_inc(S["pl0"], 16)
            sync.wait_ge(S["pl0"], 16 * 11)

        @blk.vector
        def _(vector):
            vector.memset(ones_sb[:, :], 1.0)
            vector.memset(O_sb[:, :], 0.0)
            vector.memset(cvals[:, 0:1], EPS)
            vector.memset(cvals[:, 1:2], invD)

    NT1 = TP // 128
    with nc.Block() as blk:
        @blk.sync
        def _(sync):
            for t in range(NT1):
                pr = t % 2
                if t >= 2:
                    sync.wait_ge(S["dve1"], 4 * (t - 2) + 4)
                sync.dma_start(out=xin[pr][:, :],
                               in_=x_pad[128 * t:128 * (t + 1), :]).then_inc(S["xld" + str(pr)], 16)

        @blk.vector
        def _(vector):
            for t in range(NT1):
                pr = t % 2
                vector.wait_ge(S["xld" + str(pr)], 16 * (t // 2 + 1))
                if t >= 1:
                    vector.wait_ge(S["act1"], 3 * (t - 1) + 1)
                if t >= 1:
                    vector.wait_ge(S["dve1"], 4 * t)
                vector.reduce_sum(stat[:, 0:1], xin[pr][:, :], axis=AXX).then_inc(S["dve1"], 1)
                vector.wait_ge(S["act1"], 3 * t + 3)
                vector.reciprocal(stat[:, 4:5], stat[:, 3:4]).then_inc(S["dve1"], 1)
                vector.wait_ge(S["dve1"], 4 * t + 2)
                vector.tensor_mul(stat[:, 5:6], stat[:, 1:2], stat[:, 4:5]).then_inc(S["dve1"], 1)
                if t >= 2:
                    vector.wait_ge(S["x1sp" + str(pr)], 16 * (t // 2))
                vector.wait_ge(S["dve1"], 4 * t + 3)
                vector.tensor_scalar(out=x1tm[pr][:, :], in0=xin[pr][:, :],
                                     scalar1=stat[:, 4:5], scalar2=stat[:, 5:6],
                                     op0=ALU.mult, op1=ALU.add).then_inc(S["dve1"], 1)

        @blk.scalar
        def _(scalar):
            for t in range(NT1):
                pr = t % 2
                scalar.wait_ge(S["dve1"], 4 * t + 1)
                if t >= 1:
                    scalar.wait_ge(S["act1"], 3 * t)
                scalar.mul(stat[:, 1:2], stat[:, 0:1], -invD).then_inc(S["act1"], 1)
                scalar.wait_ge(S["act1"], 3 * t + 1)
                scalar.activation(out=sq_sb[:, :], in_=xin[pr][:, :], func=AF.Square,
                                  bias=stat[:, 1:2], accum_out=stat[:, 2:3]).then_inc(S["act1"], 1)
                scalar.wait_ge(S["act1"], 3 * t + 2)
                scalar.activation(out=stat[:, 3:4], in_=stat[:, 2:3], func=AF.Sqrt,
                                  scale=cvals[:, 1:2], bias=cvals[:, 0:1]).then_inc(S["act1"], 1)
                scalar.wait_ge(S["dve1"], 4 * t + 4)
                scalar.dma_start(out=x1tm_d[128 * t:128 * (t + 1), :],
                                 in_=x1tm[pr][:, :]).then_inc(S["x1sp" + str(pr)], 16)

    with nc.Block() as blk:
        @blk.sync
        def _(sync):
            for c in range(DHC):
                pr = c % 2
                if c >= 2:
                    sync.wait_ge(S["pool_h"], c - 1)
                sync.dma_start(
                    out=wstg[pr][:, 0:D].rearrange("p (c i) -> p c i", c=DC),
                    in_=wg1T[:, 128 * c:128 * (c + 1)].rearrange("(c p) i -> p c i", p=128),
                ).then_inc(S["wg1ld" if pr == 0 else "wg1ld1"], 16)
            H1T = NT1 // 2
            H1 = 128 * H1T
            sync.wait_ge(S["x1sp0"], 16 * ((H1T + 1) // 2))
            sync.wait_ge(S["x1sp1"], 16 * (H1T // 2))
            for c in range(DC):
                sync.dma_start(out=x1fm[c][:, 0:H1], in_=x1tm_d[0:H1, 128 * c:128 * (c + 1)],
                               transpose=True).then_inc(S[f"xbs{c}"], 16)
            sync.wait_ge(S["x1sp0"], 16 * ((NT1 + 1) // 2))
            sync.wait_ge(S["x1sp1"], 16 * (NT1 // 2))
            for c in range(DC):
                sync.dma_start(out=x1fm[c][:, H1:TP], in_=x1tm_d[H1:TP, 128 * c:128 * (c + 1)],
                               transpose=True).then_inc(S[f"xbs{c}"], 16)

        @blk.vector
        def _(vector):
            for c in range(DHC):
                pr = c % 2
                vector.wait_ge(S["wg1ld" if pr == 0 else "wg1ld1"], 16 * (c // 2 + 1))
                vector.tensor_copy(out=wg1bf[c][:, :], in_=wstg[pr][:, 0:D]).then_inc(S["pool_h"], 1)

        @blk.tensor
        def _(tensor):
            for c in range(DHC):
                ps = psA if c % 2 == 0 else psB
                tensor.wait_ge(S["pool_h"], c + 1)
                if c >= 2:
                    tensor.wait_ge(S["act_h"], c - 1)
                for n in range(NLC):
                    for k in range(DC):
                        if c == 0 and n == 0:
                            tensor.wait_ge(S[f"xbs{k}"], 32)
                        inst = tensor.matmul(ps[:, NL * n:NL * (n + 1)],
                                             wg1bf[c][:, 128 * k:128 * (k + 1)],
                                             x1fm[k][:, 2 + NL * n:2 + NL * (n + 1)],
                                             start=(k == 0), stop=(k == DC - 1))
                        if k == DC - 1:
                            inst.then_inc(S["pe_h"], 1)

        @blk.scalar
        def _(scalar):
            for c in range(DHC):
                ps = psA if c % 2 == 0 else psB
                scalar.wait_ge(S["pe_h"], NLC * (c + 1))
                scalar.activation(out=hfm[c][:, :], in_=ps[:, 0:T], func=AF.Relu,
                                  bias=bg1_sb[:, c:c + 1]).then_inc(S["act_h"], 1)

    with nc.Block() as blk:
        @blk.sync
        def _(sync):
            for rc in range(RC):
                pr = rc % 2
                if rc >= 2:
                    sync.wait_ge(S["pool2"], 3 * (rc - 2) + 3)
                sync.dma_start(
                    out=wstg[pr][:, 0:D].rearrange("p (c i) -> p c i", c=DC),
                    in_=wuT[:, 128 * rc:128 * (rc + 1)].rearrange("(c p) i -> p c i", p=128),
                ).then_inc(S["wld" + str(pr)], 16)
                sync.dma_start(
                    out=wstg[pr][:, D:2 * D].rearrange("p (c i) -> p c i", c=DC),
                    in_=wvT[:, 128 * rc:128 * (rc + 1)].rearrange("(c p) i -> p c i", p=128),
                ).then_inc(S["wld" + str(pr)], 16)
                sync.dma_start(
                    out=wg2stg[pr][:, :].rearrange("p (c i) -> p c i", c=DHC),
                    in_=wg2T[:, 128 * rc:128 * (rc + 1)].rearrange("(c p) i -> p c i", p=128),
                ).then_inc(S["wld" + str(pr)], 16)
            sync.wait_ge(S["dve2"], 2 * (RC - 1) + 1)
            sync.dma_start(out=cc_in.rearrange("o (p c) -> (o p) c", p=128),
                           in_=S_sb[:, :]).then_inc(S["ccsp"], 16)

        @blk.tensor
        def _(tensor):
            for rc in range(RC):
                pr = rc % 2
                tensor.wait_ge(S["pool2"], 3 * rc + 3)
                for n in range(NLC):
                    if rc >= 1:
                        tensor.wait_ge(S["dve2v"], 4 * (rc - 1) + n + 1)
                    for k in range(DHC):
                        inst = tensor.matmul(psA[:, NL * n:NL * (n + 1)],
                                             wg2bf[pr][:, 128 * k:128 * (k + 1)],
                                             hfm[k][:, NL * n:NL * (n + 1)],
                                             start=(k == 0), stop=(k == DHC - 1))
                        if n == NLC - 1 and k == DHC - 1:
                            inst.then_inc(S["pe2"], 1)
                if rc >= 1:
                    tensor.wait_ge(S["dve2"], 2 * (rc - 1) + 1)
                for n in range(NLC):
                    for k in range(DC):
                        inst = tensor.matmul(psB[:, NL * n:NL * (n + 1)],
                                             wubf[pr][:, 128 * k:128 * (k + 1)],
                                             x1fm[k][:, 2 + NL * n:2 + NL * (n + 1)],
                                             start=(k == 0), stop=(k == DC - 1))
                        if n == NLC - 1 and k == DC - 1:
                            inst.then_inc(S["pe2"], 1)
                tensor.wait_ge(S["act2"], rc + 1)
                for n in range(NLC):
                    for k in range(DC):
                        inst = tensor.matmul(psA[:, NL * n:NL * (n + 1)],
                                             wvbf[pr][:, 128 * k:128 * (k + 1)],
                                             x1fm[k][:, 2 + NL * n:2 + NL * (n + 1)],
                                             start=(k == 0), stop=(k == DC - 1))
                        if n == NLC - 1 and k == DC - 1:
                            inst.then_inc(S["pe2"], 1)

        @blk.scalar
        def _(scalar):
            for rc in range(RC):
                pr = rc % 2
                scalar.wait_ge(S["pe2"], 3 * rc + 1)
                if rc >= 1:
                    scalar.wait_ge(S["dve2v"], 4 * (rc - 1) + 4)
                scalar.activation(out=g_sb[:, :], in_=psA[:, 0:T], func=AF.Sigmoid,
                                  bias=bg2_sb[:, rc:rc + 1]).then_inc(S["act2"], 1)
                scalar.wait_ge(S["dve2"], 2 * rc + 2)
                scalar.dma_start(out=C_d[rc, :, :], in_=C_sb[pr][:, :]).then_inc(S["sp2" + str(pr)], 16)
                scalar.wait_ge(S["dve2v"], 4 * rc + 4)
                scalar.dma_start(out=Vg_d[rc, :, :], in_=Vg_sb[pr][:, :]).then_inc(S["sp2" + str(pr)], 16)
                if 1 <= rc <= DC - 1:
                    scalar.wait_ge(S["dve3"], 3 * (rc - 1) + 3)
                    scalar.dma_start(out=t_d[rc - 1, :, :],
                                     in_=tfm0[:, :]).then_inc(S["tsp0"], 16)
            scalar.wait_ge(S["sp20"], 32 * ((RC + 1) // 2))
            scalar.wait_ge(S["sp21"], 32 * (RC // 2))

        @blk.gpsimd
        def _(gpsimd):
            gpsimd.wait_ge(S["ccsp"], 16)
            gpsimd.collective_compute(
                "AllGather", ALU.bypass,
                replica_groups=[list(range(NCORES))],
                ins=[cc_in[:, :]], outs=[cc_out[:, :]],
            ).then_inc(S["cc"])

        @blk.vector
        def _(vector):
            def cast_w(rc2):
                pr2 = rc2 % 2
                vector.wait_ge(S["wld" + str(pr2)], 48 * (rc2 // 2 + 1))
                if rc2 >= 2:
                    vector.wait_ge(S["pe2"], 3 * (rc2 - 2) + 3)
                vector.tensor_copy(out=wubf[pr2][:, :], in_=wstg[pr2][:, 0:D]).then_inc(S["pool2"], 1)
                vector.tensor_copy(out=wvbf[pr2][:, :], in_=wstg[pr2][:, D:2 * D]).then_inc(S["pool2"], 1)
                vector.tensor_copy(out=wg2bf[pr2][:, :], in_=wg2stg[pr2][:, :]).then_inc(S["pool2"], 1)
            cast_w(0)
            if RC > 1:
                cast_w(1)
            for rc in range(RC):
                pr = rc % 2
                vector.wait_ge(S["pe2"], 3 * rc + 2)
                vector.wait_ge(S["act2"], rc + 1)
                if rc >= 1:
                    vector.wait_ge(S["dve2"], 2 * rc)
                vector.scalar_tensor_tensor(out=ug_sb[:, :], in0=psB[:, 0:T],
                                            scalar=bu_sb[:, rc:rc + 1], in1=g_sb[:, :],
                                            op0=ALU.add, op1=ALU.mult,
                                            accum_out=S_sb[:, rc:rc + 1]).then_inc(S["dve2"], 1)
                if rc >= 2:
                    vector.wait_ge(S["sp2" + str(pr)], 32 * (rc // 2))
                vector.wait_ge(S["dve2"], 2 * rc + 1)
                vector.tensor_tensor_scan(out=C_sb[pr][:, :], data0=ones_sb[:, :],
                                          data1=ug_sb[:, :], initial=0.0,
                                          op0=ALU.mult, op1=ALU.add).then_inc(S["dve2"], 1)
                vector.wait_ge(S["pe2"], 3 * rc + 3)
                for j in range(NLC):
                    vector.scalar_tensor_tensor(out=Vg_sb[pr][:, NL * j:NL * (j + 1)],
                                                in0=psA[:, NL * j:NL * (j + 1)],
                                                scalar=bv_sb[:, rc:rc + 1],
                                                in1=g_sb[:, NL * j:NL * (j + 1)],
                                                op0=ALU.add, op1=ALU.mult).then_inc(S["dve2v"], 1)
                if rc + 2 < RC:
                    cast_w(rc + 2)
                c = rc
                if c < DC:
                    if c >= 1:
                        vector.wait_ge(S["tsp0"], 16 * c)
                    vector.tensor_scalar(out=tfm0[:, :], in0=x1fm[c][:, 1:T + 1],
                                         scalar1=convw_sb[:, 3 * c:3 * c + 1],
                                         scalar2=convb_sb[:, c:c + 1],
                                         op0=ALU.mult, op1=ALU.add).then_inc(S["dve3"], 1)
                    vector.wait_ge(S["dve3"], 3 * c + 1)
                    vector.scalar_tensor_tensor(out=tfm0[:, :], in0=x1fm[c][:, 2:T + 2],
                                                scalar=convw_sb[:, 3 * c + 1:3 * c + 2],
                                                in1=tfm0[:, :], op0=ALU.mult, op1=ALU.add).then_inc(S["dve3"], 1)
                    vector.wait_ge(S["dve3"], 3 * c + 2)
                    vector.scalar_tensor_tensor(out=tfm0[:, :], in0=x1fm[c][:, 3:T + 3],
                                                scalar=convw_sb[:, 3 * c + 2:3 * c + 3],
                                                in1=tfm0[:, :], op0=ALU.mult, op1=ALU.add).then_inc(S["dve3"], 1)

    PE4 = NDC + DC
    VC = {}
    dv4 = [0]
    with nc.Block() as blk:
        @blk.gpsimd
        def _(gpsimd):
            def ld_t(lt):
                pr = lt % 2
                if lt >= 2:
                    gpsimd.wait_ge(S["pe4"], PE4 * (lt - 2) + PE4)
                if lt < 2:
                    gpsimd.wait_ge(S["tsp0"], 16 * DC)
                gpsimd.dma_start(
                    out=tin[pr][:, :].rearrange("p (dc l) -> p dc l", dc=DC),
                    in_=t_d[:, :, 128 * lt:128 * (lt + 1)].rearrange("dc p l -> p dc l"),
                ).then_inc(S["tld" + str(pr)], 16)

            ld_t(0)
            if LT > 1:
                ld_t(1)
            for ch in range(RC):
                prc = ch % 2
                if ch >= 2:
                    gpsimd.wait_ge(S["pool3"], ch - 1)
                gpsimd.dma_start(out=wun_stg[prc][:, :],
                                 in_=wuT[128 * ch:128 * (ch + 1), :]).then_inc(S["wuld" + str(prc)], 16)
            for lt in range(2, LT):
                ld_t(lt)
                if lt == min(5, LT - 1):
                    gpsimd.dma_start(
                        out=gstg[0][:, :].rearrange("p (c i) -> p c i", c=DC),
                        in_=gateT[:, 0:128].rearrange("(c p) i -> p c i", p=128),
                    ).then_inc(S["gld0e"], 16)
                    gpsimd.dma_start(
                        out=ustg[0][:, :].rearrange("p (c i) -> p c i", c=DC),
                        in_=upT[:, 0:128].rearrange("(c p) i -> p c i", p=128),
                    ).then_inc(S["gld0e"], 16)

        @blk.vector
        def _(vector):
            def wun_cast(ch):
                prc = ch % 2
                vector.wait_ge(S["wuld" + str(prc)], 16 * (ch // 2 + 1))
                vector.tensor_copy(out=wun_bf[ch][:, :], in_=wun_stg[prc][:, :]).then_inc(S["pool3"], 1)

            wun_cast(0)
            if RC > 1:
                wun_cast(1)
            vector.wait_ge(S["agld"], 16)
            for r in range(NCORES):
                if r >= 1:
                    vector.wait_ge(S["dve3b"], r)
                vector.scalar_tensor_tensor(out=O_sb[:, :], in0=ag_sb[:, RC * r:RC * (r + 1)],
                                            scalar=pm_sb[:, r:r + 1], in1=O_sb[:, :],
                                            op0=ALU.mult, op1=ALU.add).then_inc(S["dve3b"], 1)
            vector.wait_ge(S["dve3b"], NCORES)

            def inc(inst, tag):
                dv4[0] += 1
                VC[tag] = dv4[0]
                inst.then_inc(S["dve4"], 1)

            def emit_G(lt):
                p3 = lt % 3
                pr = lt % 2
                vector.wait_ge(S["ld4a" + str(p3)], 16 * (lt // 3 + 1))
                vector.wait_ge(S["ld4v" + str(p3)], 16 * (lt // 3 + 1))
                if lt >= 2:
                    vector.wait_ge(S["pe4"], PE4 * (lt - 2) + NDC)
                for rc in range(RC):
                    inc(vector.scalar_tensor_tensor(
                        out=G_sb[pr][:, 128 * rc:128 * (rc + 1)],
                        in0=Cin[p3][:, 128 * rc:128 * (rc + 1)],
                        scalar=O_sb[:, rc:rc + 1],
                        in1=Vgin[p3][:, 128 * rc:128 * (rc + 1)],
                        op0=ALU.add, op1=ALU.mult), ("gmc", lt, rc))
                VC[("gm", lt)] = VC[("gmc", lt, RC - 1)]

            emit_G(0)
            if LT > 1:
                emit_G(1)
            for ch in range(2, RC):
                wun_cast(ch)
            for lt in range(LT):
                pr = lt % 2
                vector.wait_ge(S["pe4"], PE4 * lt + PE4)
                vector.wait_ge(S["ld4b" + str(pr)], 16 * (lt // 2 + 1))
                if lt >= 1:
                    vector.wait_ge(S["sp4" + str((lt - 1) % 2)], 32 * ((lt - 1) // 2 + 1))
                inc(vector.scalar_tensor_tensor(out=xp_sb[:, :], in0=(psA if pr == 0 else psB)[:, 0:D],
                                                scalar=1.0, in1=xin4[pr][:, :], op0=ALU.mult,
                                                op1=ALU.add), ("xp", lt))
                if lt + 2 < LT:
                    emit_G(lt + 2)
                vector.wait_ge(S["dve4"], VC[("xp", lt)])
                inc(vector.reduce_sum(stat[:, 0:1], xp_sb[:, :], axis=AXX), ("red", lt))
                vector.wait_ge(S["act4"], 3 * lt + 3)
                inc(vector.reciprocal(stat[:, 4:5], stat[:, 3:4]), ("rec", lt))
                vector.wait_ge(S["dve4"], VC[("rec", lt)])
                inc(vector.tensor_mul(stat[:, 5:6], stat[:, 1:2], stat[:, 4:5]), ("nb", lt))
                if lt >= 2:
                    vector.wait_ge(S["sp4" + str(pr)], 32 * (lt // 2))
                vector.wait_ge(S["dve4"], VC[("nb", lt)])
                inc(vector.tensor_scalar(out=x2tm[pr][:, :], in0=xp_sb[:, :],
                                         scalar1=stat[:, 4:5], scalar2=stat[:, 5:6],
                                         op0=ALU.mult, op1=ALU.add), ("ap", lt))

        @blk.scalar
        def _(scalar):
            scalar.wait_ge(S["dve3"], 3 * DC)
            scalar.dma_start(out=t_d[DC - 1, :, :],
                             in_=tfm0[:, :]).then_inc(S["tsp0"], 16)

            def ld_v(lt3):
                p3 = lt3 % 3
                if lt3 >= 3:
                    scalar.wait_ge(S["dve4"], VC[("gm", lt3 - 3)])
                scalar.dma_start(
                    out=Vgin[p3][:, :].rearrange("p (rc l) -> p rc l", rc=RC),
                    in_=Vg_d[:, :, 128 * lt3:128 * (lt3 + 1)].rearrange("rc p l -> p rc l"),
                ).then_inc(S["ld4v" + str(p3)], 16)

            def ld_b(lt2):
                pr2 = lt2 % 2
                if lt2 >= 2:
                    scalar.wait_ge(S["act4"], 3 * (lt2 - 2) + 2)
                scalar.dma_start(out=xin4[pr2][:, :],
                                 in_=x_pad[2 + 128 * lt2:2 + 128 * (lt2 + 1), :]).then_inc(S["ld4b" + str(pr2)], 16)

            ld_v(0)
            ld_b(0)
            if LT > 1:
                ld_v(1)
                ld_b(1)
            if LT > 2:
                ld_v(2)
            scalar.wait_ge(S["cc"], 1)
            scalar.dma_start(
                out=ag_sb[:, 0:NCORES * RC].rearrange("p (r c) -> p r c", r=NCORES),
                in_=cc_out[:, :].rearrange("r (p c) -> p r c", p=128),
            ).then_inc(S["agld"], 16)
            for lt in range(LT):
                pr = lt % 2
                if lt == min(NQ // 128, LT - 1):
                    nq = NQ // 128
                    scalar.wait_ge(S["sp40"], 32 * ((nq + 1) // 2))
                    scalar.wait_ge(S["sp41"], 32 * (nq // 2))
                    for c in range(DC):
                        scalar.dma_start(out=x2q0[c][:, :],
                                         in_=x2tm_d[0:NQ, 128 * c:128 * (c + 1)],
                                         transpose=True).then_inc(S["xbq0"], 16)
                scalar.wait_ge(S["dve4"], VC[("red", lt)])
                if lt >= 1:
                    scalar.wait_ge(S["act4"], 3 * lt)
                scalar.mul(stat[:, 1:2], stat[:, 0:1], -invD).then_inc(S["act4"], 1)
                scalar.wait_ge(S["act4"], 3 * lt + 1)
                scalar.activation(out=xin4[pr][:, :], in_=xp_sb[:, :], func=AF.Square,
                                  bias=stat[:, 1:2], accum_out=stat[:, 2:3]).then_inc(S["act4"], 1)
                scalar.wait_ge(S["act4"], 3 * lt + 2)
                scalar.activation(out=stat[:, 3:4], in_=stat[:, 2:3], func=AF.Sqrt,
                                  scale=cvals[:, 1:2], bias=cvals[:, 0:1]).then_inc(S["act4"], 1)
                if lt + 2 < LT:
                    ld_b(lt + 2)
                if lt + 3 < LT:
                    ld_v(lt + 3)
                scalar.wait_ge(S["dve4"], VC[("ap", lt)])
                scalar.dma_start(out=xp_d[128 * lt:128 * (lt + 1), :],
                                 in_=xp_sb[:, :]).then_inc(S["sp4" + str(pr)], 16)
                scalar.dma_start(out=x2tm_d[128 * lt:128 * (lt + 1), :],
                                 in_=x2tm[pr][:, :]).then_inc(S["sp4" + str(pr)], 16)
            scalar.wait_ge(S["sp40"], 32 * ((LT + 1) // 2))
            scalar.wait_ge(S["sp41"], 32 * (LT // 2))

        @blk.sync
        def _(sync):
            def ld_a(lt):
                p3 = lt % 3
                if lt >= 3:
                    sync.wait_ge(S["dve4"], VC[("gm", lt - 3)])
                sync.dma_start(
                    out=Cin[p3][:, :].rearrange("p (rc l) -> p rc l", rc=RC),
                    in_=C_d[:, :, 128 * lt:128 * (lt + 1)].rearrange("rc p l -> p rc l"),
                ).then_inc(S["ld4a" + str(p3)], 16)

            for lt in range(LT):
                ld_a(lt)


        @blk.tensor
        def _(tensor):
            for lt in range(LT):
                pr = lt % 2
                ps = psA if pr == 0 else psB
                if lt >= 2:
                    tensor.wait_ge(S["dve4"], VC[("xp", lt - 2)])
                for rc in range(RC):
                    tensor.wait_ge(S["dve4"], VC[("gmc", lt, rc)])
                    if lt == 0:
                        tensor.wait_ge(S["pool3"], rc + 1)
                    for n in range(NDC):
                        inst = tensor.matmul(ps[:, ND * n:ND * (n + 1)],
                                             G_sb[pr][:, 128 * rc:128 * (rc + 1)],
                                             wun_bf[rc][:, ND * n:ND * (n + 1)],
                                             start=(rc == 0), stop=False)
                        if rc == RC - 1:
                            inst.then_inc(S["pe4"], 1)
                tensor.wait_ge(S["tld" + str(pr)], 16 * (lt // 2 + 1))
                for dc in range(DC):
                    tensor.matmul(ps[:, 128 * dc:128 * (dc + 1)],
                                  tin[pr][:, 128 * dc:128 * (dc + 1)],
                                  ident[:, :], is_transpose=True,
                                  start=False, stop=((dc + 1) % 4 == 0 or dc == DC - 1)
                                  ).then_inc(S["pe4"], 1)

        @blk.sync
        def _(sync):
            sync.wait_ge(S["pe4"], PE4 * LT)
            sync.wait_ge(S["sp40"], 32 * ((LT + 1) // 2))
            sync.wait_ge(S["sp41"], 32 * (LT // 2))
            sync.dma_start(
                out=gstg[1][:, :].rearrange("p (c i) -> p c i", c=DC),
                in_=gateT[:, 128:256].rearrange("(c p) i -> p c i", p=128),
            ).then_inc(S["gld1"], 16)
            sync.dma_start(
                out=ustg[1][:, :].rearrange("p (c i) -> p c i", c=DC),
                in_=upT[:, 128:256].rearrange("(c p) i -> p c i", p=128),
            ).then_inc(S["gld1"], 16)
            for q in range(1, NQC):
                sem = [None, S["xbq1"], S["xb123"], S["xbq3"]][q]
                for c in range(DC):
                    sync.dma_start(out=x2fm[c][:, NQ * q:NQ * (q + 1)],
                                   in_=x2tm_d[NQ * q:NQ * (q + 1), 128 * c:128 * (c + 1)],
                                   transpose=True).then_inc(sem, 16)

        @blk.sync
        def _(sync):
            for ic in range(2, IC):
                pr = ic % 2
                sync.wait_ge(S["pool5"], 2 * (ic - 2) + 2)
                sync.dma_start(
                    out=gstg[pr][:, :].rearrange("p (c i) -> p c i", c=DC),
                    in_=gateT[:, 128 * ic:128 * (ic + 1)].rearrange("(c p) i -> p c i", p=128),
                ).then_inc(S["gld" + str(pr)], 16)
                sync.dma_start(
                    out=ustg[pr][:, :].rearrange("p (c i) -> p c i", c=DC),
                    in_=upT[:, 128 * ic:128 * (ic + 1)].rearrange("(c p) i -> p c i", p=128),
                ).then_inc(S["gld" + str(pr)], 16)

        @blk.tensor
        def _(tensor):
            def x2src(k, n):
                if n == 0:
                    return x2q0[k][:, :]
                return x2fm[k][:, NL * n:NL * (n + 1)]

            for ic in range(IC):
                pr = ic % 2
                tensor.wait_ge(S["pool5"], 2 * ic + 2)
                if ic >= 1:
                    tensor.wait_ge(S["act5"], ic)
                for n in range(NLC):
                    if ic == 0:
                        sem = [S["xbq0"], S["xbq1"], S["xb123"], S["xbq3"]][n]
                        tensor.wait_ge(sem, 16 * DC)
                    for k in range(DC):
                        inst = tensor.matmul(psA[:, NL * n:NL * (n + 1)],
                                             gbf[pr][:, 128 * k:128 * (k + 1)],
                                             x2src(k, n),
                                             start=(k == 0), stop=(k == DC - 1))
                        if n == NLC - 1 and k == DC - 1:
                            inst.then_inc(S["pe5"], 1)
                if ic >= 1:
                    tensor.wait_ge(S["dve5"], ic)
                for n in range(NLC):
                    for k in range(DC):
                        inst = tensor.matmul(psB[:, NL * n:NL * (n + 1)],
                                             ubf[pr][:, 128 * k:128 * (k + 1)],
                                             x2src(k, n),
                                             start=(k == 0), stop=(k == DC - 1))
                        if n == NLC - 1 and k == DC - 1:
                            inst.then_inc(S["pe5"], 1)

        @blk.scalar
        def _(scalar):
            for ic in range(IC):
                pr = ic % 2
                scalar.wait_ge(S["pe5"], 2 * ic + 1)
                if ic >= 1:
                    scalar.wait_ge(S["dve5"], ic)
                scalar.activation(out=silu_sb[:, :], in_=psA[:, 0:T], func=AF.Silu,
                                  scale=gs_sb[:, ic:ic + 1]).then_inc(S["act5"], 1)
                if ic >= 1:
                    scalar.dma_start(out=hf_d[ic - 1, :, :],
                                     in_=hf_sb[(ic - 1) % 2][:, :]).then_inc(S["hsp" + str((ic - 1) % 2)], 16)
            scalar.wait_ge(S["dve5"], IC)
            scalar.dma_start(out=hf_d[IC - 1, :, :],
                             in_=hf_sb[(IC - 1) % 2][:, :]).then_inc(S["hsp" + str((IC - 1) % 2)], 16)
            scalar.wait_ge(S["hsp0"], 16 * ((IC + 1) // 2))
            scalar.wait_ge(S["hsp1"], 16 * (IC // 2))

        @blk.vector
        def _(vector):
            def deq(ic2):
                pr2 = ic2 % 2
                if ic2 == 0:
                    vector.wait_ge(S["gld0e"], 32)
                elif pr2 == 0:
                    vector.wait_ge(S["gld0"], 32 * (ic2 // 2))
                else:
                    vector.wait_ge(S["gld1"], 32 * (ic2 // 2 + 1))
                if ic2 >= 2:
                    vector.wait_ge(S["pe5"], 2 * (ic2 - 2) + 2)
                vector.tensor_scalar(out=gbf[pr2][:, :], in0=gstg[pr2][:, :], scalar1=-128.0,
                                     scalar2=None, op0=ALU.add).then_inc(S["pool5"], 1)
                vector.tensor_scalar(out=ubf[pr2][:, :], in0=ustg[pr2][:, :], scalar1=-128.0,
                                     scalar2=None, op0=ALU.add).then_inc(S["pool5"], 1)
            vector.wait_ge(S["sp40"], 32 * ((LT + 1) // 2))
            vector.wait_ge(S["sp41"], 32 * (LT // 2))
            deq(0)
            if IC > 1:
                deq(1)
            for ic in range(IC):
                pr = ic % 2
                if ic + 2 < IC:
                    deq(ic + 2)
                vector.wait_ge(S["pe5"], 2 * ic + 2)
                vector.wait_ge(S["act5"], ic + 1)
                if ic >= 2:
                    vector.wait_ge(S["hsp" + str(pr)], 16 * (ic // 2))
                vector.scalar_tensor_tensor(out=hf_sb[pr][:, :], in0=psB[:, 0:T],
                                            scalar=us_sb[:, ic:ic + 1], in1=silu_sb[:, :],
                                            op0=ALU.mult, op1=ALU.mult).then_inc(S["dve5"], 1)

        @blk.sync
        def _(sync):
            hg = [S["hga"], S["hgb"]]
            sync.wait_ge(S["hsp0"], 16 * ((IC + 1) // 2))
            sync.wait_ge(S["hsp1"], 16 * (IC // 2))
            for g in range(IC // 8):
                if g >= 2:
                    sync.wait_ge(hg[g % 2], 128 * (g // 2))
                for j in range(8):
                    ic = 8 * g + j
                    sync.dma_start(out=hfh[ic][:, :],
                                   in_=hf_d[ic, :, 0:TH]).then_inc(hg[g % 2], 16)

        @blk.gpsimd
        def _(gpsimd):
            gpsimd.wait_ge(S["pe4"], PE4 * LT)
            for lh in range(LH):
                for dc in range(NDC):
                    for ic in range(IC):
                        seq = (lh * NDC + dc) * IC + ic
                        pr = seq % 4
                        if seq >= 4:
                            gpsimd.wait_ge(S["pool6"], seq - 3)
                        gpsimd.dma_start(out=dstg[pr][:, :],
                                         in_=downT[128 * ic:128 * (ic + 1),
                                                   ND * dc:ND * (dc + 1)]).then_inc(S["dld" + str(pr)], 16)

        @blk.tensor
        def _(tensor):
            hg = [S["hga"], S["hgb"]]
            for lh in range(LH):
                for dc in range(NDC):
                    dcseq = lh * NDC + dc
                    for ic in range(IC):
                        if dc == 0 and ic % 8 == 0:
                            g = lh * (IC // 8) + ic // 8
                            tensor.wait_ge(hg[g % 2], 128 * (g // 2 + 1))
                        seq = dcseq * IC + ic
                        pr = seq % 4
                        tensor.wait_ge(S["pool6"], seq + 1)
                        last = None
                        for lt in range(LTH):
                            if ic == 0 and dcseq >= 1:
                                tensor.wait_ge(S["dve6"], (dcseq - 1) * 2 * LTH + 2 * (lt + 1))
                            ps = psall[lt // 4] if LTH > 4 else psall[0]
                            slc = ps[:, 512 * (lt % 4):512 * (lt % 4) + ND]
                            last = tensor.matmul(slc,
                                                 hfh[ic][:, 128 * lt:128 * (lt + 1)],
                                                 dbf[pr][:, :],
                                                 start=(ic == 0), stop=(ic == IC - 1))
                        last.then_inc(S["pe6"], 1)

        @blk.scalar
        def _(scalar):
            for lh in range(LH):
                for dc in range(NDC):
                    dcseq = lh * NDC + dc
                    pr = dcseq % 2
                    if dcseq >= 2:
                        scalar.wait_ge(S["dve6"], (dcseq - 1) * 2 * LTH)
                    scalar.dma_start(
                        out=xr_sb[pr][:, :].rearrange("p (lt d) -> p lt d", lt=LTH),
                        in_=xp_d[TH * lh:TH * (lh + 1), ND * dc:ND * (dc + 1)].rearrange("(lt p) d -> p lt d", p=128),
                    ).then_inc(S["xrld" if pr == 0 else "xrld1"], 16)
                    if lh == 0 and dc == NDC - 1:
                        hg = [S["hga"], S["hgb"]]
                        for ic in range(IC):
                            if ic % 8 == 0:
                                g = (IC // 8) + ic // 8
                                scalar.wait_ge(hg[g % 2], 128 * (g // 2))
                            scalar.wait_ge(S["pe6"], (NDC - 1) * IC + ic + 1)
                            scalar.dma_start(out=hfh[ic][:, :],
                                             in_=hf_d[ic, :, TH:2 * TH]).then_inc(hg[g % 2], 16)
                    for lt in range(LTH):
                        oseq = dcseq * LTH + lt
                        scalar.wait_ge(S["dve6"], dcseq * 2 * LTH + 2 * (lt + 1))
                        scalar.dma_start(
                            out=out_d[TH * lh + 128 * lt:TH * lh + 128 * (lt + 1),
                                      ND * dc:ND * (dc + 1)],
                            in_=outb[oseq % 4][:, :],
                        ).then_inc(S["osp" + str(oseq % 4)], 16)

        @blk.vector
        def _(vector):
            def deq6(seq):
                pr4 = seq % 4
                vector.wait_ge(S["dld" + str(pr4)], 16 * (seq // 4 + 1))
                if seq >= 4:
                    vector.wait_ge(S["pe6"], seq - 3)
                vector.tensor_scalar(out=dbf[pr4][:, :], in0=dstg[pr4][:, :], scalar1=-128.0,
                                     scalar2=None, op0=ALU.add).then_inc(S["pool6"], 1)

            def evacs(dcseq):
                dc = dcseq % NDC
                pr = dcseq % 2
                vector.wait_ge(S["xrld" if pr == 0 else "xrld1"], 16 * (dcseq // 2 + 1))
                vector.wait_ge(S["pe6"], (dcseq + 1) * IC)
                for lt in range(LTH):
                    oseq = dcseq * LTH + lt
                    ps = psall[lt // 4] if LTH > 4 else psall[0]
                    slc = ps[:, 512 * (lt % 4):512 * (lt % 4) + ND]
                    ob = outb[oseq % 4]
                    if oseq >= 4:
                        vector.wait_ge(S["osp" + str(oseq % 4)], 16 * (oseq // 4))
                    vector.tensor_mul(out=ob[:, :], in0=slc,
                                      in1=sd_sb[:, ND * dc:ND * (dc + 1)]).then_inc(S["dve6"], 1)
                    vector.wait_ge(S["dve6"], dcseq * 2 * LTH + 2 * lt + 1)
                    vector.tensor_add(out=ob[:, :], in0=ob[:, :],
                                      in1=xr_sb[pr][:, lt * ND:(lt + 1) * ND]).then_inc(S["dve6"], 1)

            NSEQ = LH * NDC
            for dcseq in range(NSEQ):
                for ic in range(4):
                    deq6(dcseq * IC + ic)
                if dcseq >= 1:
                    evacs(dcseq - 1)
                for ic in range(4, IC):
                    deq6(dcseq * IC + ic)
            evacs(NSEQ - 1)

        @blk.sync
        def _(sync):
            total = LH * NDC * LTH
            for j in range(4):
                sync.wait_ge(S["osp" + str(j)], 16 * ((total - j + 3) // 4))

    return nc


def _prep_inputs(L0_attn_x, L0_attn_Wu, L0_attn_bu, L0_attn_Wv, L0_attn_bv,
                 L0_attn_Wg1, L0_attn_bg1, L0_attn_Wg2, L0_attn_bg2,
                 L0_attn_conv_w, L0_attn_conv_b, L0_ffn_gate_w, L0_ffn_gate_scale,
                 L0_ffn_up_w, L0_ffn_up_scale, L0_ffn_down_w, L0_ffn_down_scale,
                 cfg=FULL, ncores=NCORES):
    T, D, R, DH, I = cfg["T"], cfg["D"], cfg["R"], cfg["DH"], cfg["I"]
    TP = T + 128
    RCn, DHCn, DCn, ICn = R // 128, DH // 128, D // 128, I // 128
    x = np.asarray(L0_attn_x, dtype=np.float32).reshape(-1, D)
    Ltot = x.shape[0]
    f32c = lambda a: np.ascontiguousarray(np.asarray(a, dtype=np.float32))
    u8c = lambda a: np.ascontiguousarray(np.asarray(a).astype(np.uint8))

    shared = dict(
        wuT=f32c(np.asarray(L0_attn_Wu).T),
        wvT=f32c(np.asarray(L0_attn_Wv).T),
        wg1T=f32c(np.asarray(L0_attn_Wg1).T),
        wg2T=f32c(np.asarray(L0_attn_Wg2).T),
        bu_r=f32c(np.asarray(L0_attn_bu).reshape(RCn, 128).T),
        bv_r=f32c(np.asarray(L0_attn_bv).reshape(RCn, 128).T),
        bg1_r=f32c(np.asarray(L0_attn_bg1).reshape(DHCn, 128).T),
        bg2_r=f32c(np.asarray(L0_attn_bg2).reshape(RCn, 128).T),
        convw_r=f32c(np.asarray(L0_attn_conv_w).reshape(D, 3).reshape(DCn, 128, 3)
                     .transpose(1, 0, 2).reshape(128, 3 * DCn)),
        convb_r=f32c(np.asarray(L0_attn_conv_b).reshape(DCn, 128).T),
        gateT=u8c(np.asarray(L0_ffn_gate_w).T),
        upT=u8c(np.asarray(L0_ffn_up_w).T),
        downT=u8c(np.asarray(L0_ffn_down_w).T),
        gs_r=f32c(np.asarray(L0_ffn_gate_scale).reshape(ICn, 128).T),
        us_r=f32c(np.asarray(L0_ffn_up_scale).reshape(ICn, 128).T),
        sd_b=f32c(np.broadcast_to(np.asarray(L0_ffn_down_scale, dtype=np.float32), (128, D))),
        ident_in=np.eye(128, dtype=np.float32),
    )
    in_maps = []
    Tcore = Ltot // ncores
    assert Tcore == T, (Tcore, T)
    for core in range(ncores):
        s = core * T
        xp = np.zeros((TP, D), dtype=np.float32)
        lo = max(s - 1, 0)
        hi = min(s + T + 1, Ltot)
        xp[2 + (lo - s):2 + (hi - s), :] = x[lo:hi]
        pm = np.zeros((128, ncores), dtype=np.float32)
        pm[:, :core] = 1.0
        m = dict(shared)
        m["x_pad"] = xp
        m["premask"] = pm
        in_maps.append(m)
    return in_maps


def kernel(**inputs):
    cfg = FULL
    in_maps = _prep_inputs(cfg=cfg, ncores=NCORES, **inputs)
    nc = build_nc(**cfg)
    res = run_bass_kernel_spmd(nc, in_maps, core_ids=list(range(NCORES)))
    out = np.concatenate([r["out_d"] for r in res.results], axis=0)
    return out.reshape(1, NCORES * cfg["T"], cfg["D"]).astype(np.float32)


# version 31 (final)
# speedup vs baseline: 1.0055x; 1.0020x over previous
import numpy as np
import concourse.bass as bass
import concourse.mybir as mybir
from concourse.bass_utils import run_bass_kernel_spmd

F32 = mybir.dt.float32
BF16 = mybir.dt.bfloat16
I32 = mybir.dt.int32
U8 = mybir.dt.uint8
AF = mybir.ActivationFunctionType
ALU = mybir.AluOpType
AXX = mybir.AxisListType.X

NCORES = 8
EPS = 1e-5

FULL = dict(T=2048, D=2048, R=2048, DH=512, I=8192)


def build_nc(T=2048, D=2048, R=2048, DH=512, I=8192):
    TP = T + 128
    LT = T // 128
    DC = D // 128
    RC = R // 128
    DHC = DH // 128
    IC = I // 128
    NL = min(512, T); NLC = T // NL
    NQC = NLC
    ND = min(512, D); NDC = D // ND
    LH = 2
    TH = T // LH
    LTH = TH // 128
    invD = 1.0 / D

    nc = bass.Bass("TRN2", target_bir_lowering=False)

    x_pad = nc.dram_tensor("x_pad", [TP, D], F32, kind="ExternalInput").ap()
    wuT = nc.dram_tensor("wuT", [D, R], F32, kind="ExternalInput").ap()
    wvT = nc.dram_tensor("wvT", [D, R], F32, kind="ExternalInput").ap()
    wg1T = nc.dram_tensor("wg1T", [D, DH], F32, kind="ExternalInput").ap()
    wg2T = nc.dram_tensor("wg2T", [DH, R], F32, kind="ExternalInput").ap()
    bu_r = nc.dram_tensor("bu_r", [128, RC], F32, kind="ExternalInput").ap()
    bv_r = nc.dram_tensor("bv_r", [128, RC], F32, kind="ExternalInput").ap()
    bg1_r = nc.dram_tensor("bg1_r", [128, DHC], F32, kind="ExternalInput").ap()
    bg2_r = nc.dram_tensor("bg2_r", [128, RC], F32, kind="ExternalInput").ap()
    convw_r = nc.dram_tensor("convw_r", [128, 3 * DC], F32, kind="ExternalInput").ap()
    convb_r = nc.dram_tensor("convb_r", [128, DC], F32, kind="ExternalInput").ap()
    gateT = nc.dram_tensor("gateT", [D, I], U8, kind="ExternalInput").ap()
    upT = nc.dram_tensor("upT", [D, I], U8, kind="ExternalInput").ap()
    downT = nc.dram_tensor("downT", [I, D], U8, kind="ExternalInput").ap()
    gs_r = nc.dram_tensor("gs_r", [128, IC], F32, kind="ExternalInput").ap()
    us_r = nc.dram_tensor("us_r", [128, IC], F32, kind="ExternalInput").ap()
    sd_b = nc.dram_tensor("sd_b", [128, D], F32, kind="ExternalInput").ap()
    premask = nc.dram_tensor("premask", [128, NCORES], F32, kind="ExternalInput").ap()
    ident_in = nc.dram_tensor("ident_in", [128, 128], F32, kind="ExternalInput").ap()

    out_d = nc.dram_tensor("out_d", [T, D], F32, kind="ExternalOutput").ap()

    x1tm_d = nc.dram_tensor("x1tm_d", [TP, D], BF16, kind="Internal").ap()
    x2tm_d = nc.dram_tensor("x2tm_d", [T, D], BF16, kind="Internal").ap()
    C_d = nc.dram_tensor("C_d", [RC, 128, T], BF16, kind="Internal").ap()
    Vg_d = nc.dram_tensor("Vg_d", [RC, 128, T], BF16, kind="Internal").ap()
    t_d = nc.dram_tensor("t_d", [DC, 128, T], F32, kind="Internal").ap()
    xp_d = nc.dram_tensor("xp_d", [T, D], F32, kind="Internal").ap()
    hf_d = nc.dram_tensor("hf_d", [IC, 128, T], BF16, kind="Internal").ap()
    cc_in = nc.dram_tensor("cc_in", [1, R], F32, kind="Internal").ap()
    cc_out = nc.dram_tensor("cc_out", [NCORES, R], F32, kind="Internal").ap()

    f32b = lambda n: (4 * n + 31) // 32 * 32
    bfb = lambda n: (2 * n + 31) // 32 * 32
    BASE = (nc._sbuf_addr_for_side("left") + 31) // 32 * 32
    TOPA = nc._sbuf_addr_for_side("right")
    LIMIT = TOPA - BASE - 3584
    nc.alloc_sbuf_tensor("manual_guard", [128, LIMIT], mybir.dt.uint8)
    _dtb = {F32: 4, BF16: 2, I32: 4, U8: 1}

    def at(name, shape, dtype, offset):
        offset = BASE + (offset + 31) // 32 * 32
        assert offset + shape[-1] * _dtb[dtype] <= BASE + LIMIT, name
        return nc.alloc_sbuf_tensor_at(name, shape, dtype, offset=offset)

    p = 0
    bu_sb = at("bu_sb", [128, RC], F32, offset=p); p += f32b(RC)
    bv_sb = at("bv_sb", [128, RC], F32, offset=p); p += f32b(RC)
    bg1_sb = at("bg1_sb", [128, DHC], F32, offset=p); p += f32b(DHC)
    bg2_sb = at("bg2_sb", [128, RC], F32, offset=p); p += f32b(RC)
    convw_sb = at("convw_sb", [128, 3 * DC], F32, offset=p); p += f32b(3 * DC)
    convb_sb = at("convb_sb", [128, DC], F32, offset=p); p += f32b(DC)
    gs_sb = at("gs_sb", [128, IC], F32, offset=p); p += f32b(IC)
    us_sb = at("us_sb", [128, IC], F32, offset=p); p += f32b(IC)
    sd_sb = at("sd_sb", [128, D], F32, offset=p); p += f32b(D)
    pm_sb = at("pm_sb", [128, NCORES], F32, offset=p); p += f32b(NCORES)
    ident = at("ident", [128, 128], F32, offset=p); p += f32b(128)
    ones_sb = at("ones_sb", [128, T], F32, offset=p); p += f32b(T)
    S_sb = at("S_sb", [128, RC], F32, offset=p); p += f32b(RC)
    O_sb = at("O_sb", [128, RC], F32, offset=p); p += f32b(RC)
    ag_sb = at("ag_sb", [128, NCORES * RC], F32, offset=p); p += f32b(NCORES * RC)
    stat = at("stat", [128, 12], F32, offset=p); p += f32b(12)
    cvals = at("cvals", [128, 2], F32, offset=p); p += f32b(2)
    PERM_END = (p + 63) // 64 * 64
    assert PERM_END <= 24 * 1024, PERM_END

    X1W = TP
    x1fm = [at(f"x1fm{c}", [128, X1W], BF16, offset=PERM_END + c * bfb(X1W)) for c in range(DC)]
    X1_END = PERM_END + DC * bfb(X1W)
    hfm = [at(f"hfm{c}", [128, T], BF16, offset=X1_END + c * bfb(T)) for c in range(DHC)]
    H_END = X1_END + DHC * bfb(T)

    p = H_END
    xin = [at(f"xin{j}", [128, D], F32, offset=p + j * f32b(D)) for j in range(2)]; p += 2 * f32b(D)
    sq_sb = at("sq_sb", [128, D], F32, offset=p); p += f32b(D)
    x1tm = [at(f"x1tm{j}", [128, D], BF16, offset=p + j * bfb(D)) for j in range(2)]; p += 2 * bfb(D)
    assert p <= LIMIT, p

    p = H_END
    wstg = [at(f"wstg{j}", [128, 2 * D], F32, offset=p + j * f32b(2 * D)) for j in range(2)]; p += 2 * f32b(2 * D)
    wubf = [at(f"wubf{j}", [128, D], BF16, offset=p + j * bfb(D)) for j in range(2)]; p += 2 * bfb(D)
    wvbf = [at(f"wvbf{j}", [128, D], BF16, offset=p + j * bfb(D)) for j in range(2)]; p += 2 * bfb(D)
    wg2stg = [at(f"wg2stg{j}", [128, DH], F32, offset=p + j * f32b(DH)) for j in range(2)]; p += 2 * f32b(DH)
    wg2bf = [at(f"wg2bf{j}", [128, DH], BF16, offset=p + j * bfb(DH)) for j in range(2)]; p += 2 * bfb(DH)
    g_sb = at("g_sb", [128, T], F32, offset=p); p += f32b(T)
    ug_sb = at("ug_sb", [128, T], F32, offset=p); p += f32b(T)
    wg1bf = [at(f"wg1bf{c}", [128, D], BF16, offset=p + c * bfb(D)) for c in range(DHC)]
    C_sb = [at(f"C_sb{j}", [128, T], BF16, offset=p + j * bfb(T)) for j in range(2)]; p += 2 * bfb(T)
    Vg_sb = [at(f"Vg_sb{j}", [128, T], BF16, offset=p + j * bfb(T)) for j in range(2)]; p += 2 * bfb(T)
    tfm0 = at("tfm0", [128, T], F32, offset=p)
    TFM_END = p + f32b(T)
    assert TFM_END <= LIMIT, p
    assert DHC * bfb(D) <= 2 * f32b(T) + 2 * bfb(T)

    p = H_END
    wun_stg = [at(f"wun_stg{j}", [128, D], F32, offset=p + j * f32b(D)) for j in range(2)]; p += 2 * f32b(D)
    WUN_START = p
    wun_bf = [at(f"wun_bf{c}", [128, D], BF16, offset=p + c * bfb(D)) for c in range(RC)]; p += RC * bfb(D)
    assert p <= LIMIT, p
    PH3_END = p

    NQ = min(512, T)
    p = PH3_END
    x2q0 = [at(f"x2q0{c}", [128, NQ], BF16, offset=p + c * bfb(NQ)) for c in range(DC)]; p += DC * bfb(NQ)
    gstg0_e = at("gstg0_e", [128, D], U8, offset=p); p += (D + 31) // 32 * 32
    ustg0_e = at("ustg0_e", [128, D], U8, offset=p); p += (D + 31) // 32 * 32
    assert p <= LIMIT, p

    sz4 = (3 * bfb(R) + 3 * bfb(R) + 2 * f32b(D) + 2 * f32b(D) + 2 * bfb(R)
           + f32b(D) + 2 * bfb(D))
    p = PERM_END if PERM_END + sz4 <= WUN_START else max(PH3_END, TFM_END)
    Cin = [at(f"Cin{j}", [128, R], BF16, offset=p + j * bfb(R)) for j in range(3)]; p += 3 * bfb(R)
    Vgin = [at(f"Vgin{j}", [128, R], BF16, offset=p + j * bfb(R)) for j in range(3)]; p += 3 * bfb(R)
    tin = [at(f"tin{j}", [128, D], F32, offset=p + j * f32b(D)) for j in range(2)]; p += 2 * f32b(D)
    xin4 = [at(f"xin4{j}", [128, D], F32, offset=p + j * f32b(D)) for j in range(2)]; p += 2 * f32b(D)
    G_sb = [at(f"G_sb{j}", [128, R], BF16, offset=p + j * bfb(R)) for j in range(2)]; p += 2 * bfb(R)
    xp_sb = at("xp_sb", [128, D], F32, offset=p); p += f32b(D)
    x2tm = [at(f"x2tm{j}", [128, D], BF16, offset=p + j * bfb(D)) for j in range(2)]; p += 2 * bfb(D)
    assert p <= LIMIT and (p <= WUN_START or p > PH3_END), (p, WUN_START, PH3_END)

    p = PERM_END
    u8b = lambda n: (n + 31) // 32 * 32
    x2fm = [at(f"x2fm{c}", [128, T], BF16, offset=p + c * bfb(T)) for c in range(DC)]; p += DC * bfb(T)
    gstg = [gstg0_e, at("gstg1", [128, D], U8, offset=p)]; p += u8b(D)
    ustg = [ustg0_e, at("ustg1", [128, D], U8, offset=p)]; p += u8b(D)
    gbf = [at(f"gbf{j}", [128, D], BF16, offset=p + j * bfb(D)) for j in range(2)]; p += 2 * bfb(D)
    ubf = [at(f"ubf{j}", [128, D], BF16, offset=p + j * bfb(D)) for j in range(2)]; p += 2 * bfb(D)
    silu_sb = at("silu_sb", [128, T], F32, offset=p); p += f32b(T)
    hf_sb = [at(f"hf_sb{j}", [128, T], BF16, offset=p + j * bfb(T)) for j in range(2)]; p += 2 * bfb(T)
    assert p <= LIMIT, p

    p = PERM_END
    hfh = [at(f"hfh{c}", [128, TH], BF16, offset=p + c * bfb(TH)) for c in range(IC)]; p += IC * bfb(TH)
    dstg = [at(f"dstg{j}", [128, ND], U8, offset=p + j * u8b(ND)) for j in range(4)]; p += 4 * u8b(ND)
    dbf = [at(f"dbf{j}", [128, ND], BF16, offset=p + j * bfb(ND)) for j in range(4)]; p += 4 * bfb(ND)
    xr_sb = [at(f"xr_sb{j}", [128, LTH * ND], F32, offset=p + j * f32b(LTH * ND)) for j in range(2)]; p += 2 * f32b(LTH * ND)
    outb = [at(f"outb{j}", [128, ND], F32, offset=p + j * f32b(ND)) for j in range(4)]; p += 4 * f32b(ND)
    assert p <= LIMIT, p

    psA = nc.alloc_psum_tensor("psA", [128, 2048], F32)
    psB = nc.alloc_psum_tensor("psB", [128, 2048], F32)
    psall = [psA, psB]

    S = {}
    for n in ("pl0", "xld0", "xld1", "dve1", "act1", "x1sp0", "x1sp1",
              "xb1", "wg1ld", "wg1ld1", "pool_h", "pe_h", "act_h",
              "wld0", "wld1", "pool2", "pe2", "act2", "dve2", "sp20", "sp21",
              "wuld0", "wuld1", "pool3", "act3", "dve3", "tsp0", "tsp1",
              "ccsp", "cc", "agld", "dve3b",
              "ld4a0", "ld4a1", "ld4a2", "ld4v0", "ld4v1", "ld4v2",
              "tld0", "tld1", "ld4b0", "ld4b1", "dve4", "act4", "pe4", "sp40", "sp41",
              "xbq0", "xbq1", "xbq3", "xb123", "dve2v",
              "gld0", "gld0e", "gld1", "pool5", "pe5", "act5", "dve5", "hsp0", "hsp1",
              "hga", "hgb", "dld0", "dld1", "dld2", "dld3", "pool6", "pe6", "dve6", "xrld", "xrld1", "osp0", "osp1", "osp2", "osp3"):
        S[n] = nc.alloc_semaphore(n)
    for c in range(D // 128):
        S[f"xbs{c}"] = nc.alloc_semaphore(f"xbs{c}")

    with nc.Block() as blk:
        @blk.sync
        def _(sync):
            sync.dma_start(out=bu_sb[:, :], in_=bu_r[:, :]).then_inc(S["pl0"], 16)
            sync.dma_start(out=bv_sb[:, :], in_=bv_r[:, :]).then_inc(S["pl0"], 16)
            sync.dma_start(out=bg1_sb[:, :], in_=bg1_r[:, :]).then_inc(S["pl0"], 16)
            sync.dma_start(out=bg2_sb[:, :], in_=bg2_r[:, :]).then_inc(S["pl0"], 16)
            sync.dma_start(out=convw_sb[:, :], in_=convw_r[:, :]).then_inc(S["pl0"], 16)
            sync.dma_start(out=convb_sb[:, :], in_=convb_r[:, :]).then_inc(S["pl0"], 16)
            sync.dma_start(out=gs_sb[:, :], in_=gs_r[:, :]).then_inc(S["pl0"], 16)
            sync.dma_start(out=us_sb[:, :], in_=us_r[:, :]).then_inc(S["pl0"], 16)
            sync.dma_start(out=sd_sb[:, :], in_=sd_b[:, :]).then_inc(S["pl0"], 16)
            sync.dma_start(out=pm_sb[:, :], in_=premask[:, :]).then_inc(S["pl0"], 16)
            sync.dma_start(out=ident[:, :], in_=ident_in[:, :]).then_inc(S["pl0"], 16)
            sync.wait_ge(S["pl0"], 16 * 11)

        @blk.vector
        def _(vector):
            vector.memset(ones_sb[:, :], 1.0)
            vector.memset(O_sb[:, :], 0.0)
            vector.memset(cvals[:, 0:1], EPS)
            vector.memset(cvals[:, 1:2], invD)

    NT1 = TP // 128
    with nc.Block() as blk:
        @blk.sync
        def _(sync):
            for t in range(NT1):
                pr = t % 2
                if t >= 2:
                    sync.wait_ge(S["dve1"], 4 * (t - 2) + 4)
                sync.dma_start(out=xin[pr][:, :],
                               in_=x_pad[128 * t:128 * (t + 1), :]).then_inc(S["xld" + str(pr)], 16)

        @blk.vector
        def _(vector):
            for t in range(NT1):
                pr = t % 2
                vector.wait_ge(S["xld" + str(pr)], 16 * (t // 2 + 1))
                if t >= 1:
                    vector.wait_ge(S["act1"], 3 * (t - 1) + 1)
                if t >= 1:
                    vector.wait_ge(S["dve1"], 4 * t)
                vector.reduce_sum(stat[:, 0:1], xin[pr][:, :], axis=AXX).then_inc(S["dve1"], 1)
                vector.wait_ge(S["act1"], 3 * t + 3)
                vector.reciprocal(stat[:, 4:5], stat[:, 3:4]).then_inc(S["dve1"], 1)
                vector.wait_ge(S["dve1"], 4 * t + 2)
                vector.tensor_mul(stat[:, 5:6], stat[:, 1:2], stat[:, 4:5]).then_inc(S["dve1"], 1)
                if t >= 2:
                    vector.wait_ge(S["x1sp" + str(pr)], 16 * (t // 2))
                vector.wait_ge(S["dve1"], 4 * t + 3)
                vector.tensor_scalar(out=x1tm[pr][:, :], in0=xin[pr][:, :],
                                     scalar1=stat[:, 4:5], scalar2=stat[:, 5:6],
                                     op0=ALU.mult, op1=ALU.add).then_inc(S["dve1"], 1)

        @blk.scalar
        def _(scalar):
            for t in range(NT1):
                pr = t % 2
                scalar.wait_ge(S["dve1"], 4 * t + 1)
                if t >= 1:
                    scalar.wait_ge(S["act1"], 3 * t)
                scalar.mul(stat[:, 1:2], stat[:, 0:1], -invD).then_inc(S["act1"], 1)
                scalar.wait_ge(S["act1"], 3 * t + 1)
                scalar.activation(out=sq_sb[:, :], in_=xin[pr][:, :], func=AF.Square,
                                  bias=stat[:, 1:2], accum_out=stat[:, 2:3]).then_inc(S["act1"], 1)
                scalar.wait_ge(S["act1"], 3 * t + 2)
                scalar.activation(out=stat[:, 3:4], in_=stat[:, 2:3], func=AF.Sqrt,
                                  scale=cvals[:, 1:2], bias=cvals[:, 0:1]).then_inc(S["act1"], 1)
                scalar.wait_ge(S["dve1"], 4 * t + 4)
                scalar.dma_start(out=x1tm_d[128 * t:128 * (t + 1), :],
                                 in_=x1tm[pr][:, :]).then_inc(S["x1sp" + str(pr)], 16)

    with nc.Block() as blk:
        @blk.sync
        def _(sync):
            for c in range(DHC):
                pr = c % 2
                if c >= 2:
                    sync.wait_ge(S["pool_h"], c - 1)
                sync.dma_start(
                    out=wstg[pr][:, 0:D].rearrange("p (c i) -> p c i", c=DC),
                    in_=wg1T[:, 128 * c:128 * (c + 1)].rearrange("(c p) i -> p c i", p=128),
                ).then_inc(S["wg1ld" if pr == 0 else "wg1ld1"], 16)
            H1T = NT1 // 2
            H1 = 128 * H1T
            sync.wait_ge(S["x1sp0"], 16 * ((H1T + 1) // 2))
            sync.wait_ge(S["x1sp1"], 16 * (H1T // 2))
            for c in range(DC):
                sync.dma_start(out=x1fm[c][:, 0:H1], in_=x1tm_d[0:H1, 128 * c:128 * (c + 1)],
                               transpose=True).then_inc(S[f"xbs{c}"], 16)
            sync.wait_ge(S["x1sp0"], 16 * ((NT1 + 1) // 2))
            sync.wait_ge(S["x1sp1"], 16 * (NT1 // 2))
            for c in range(DC):
                sync.dma_start(out=x1fm[c][:, H1:TP], in_=x1tm_d[H1:TP, 128 * c:128 * (c + 1)],
                               transpose=True).then_inc(S[f"xbs{c}"], 16)

        @blk.vector
        def _(vector):
            for c in range(DHC):
                pr = c % 2
                vector.wait_ge(S["wg1ld" if pr == 0 else "wg1ld1"], 16 * (c // 2 + 1))
                vector.tensor_copy(out=wg1bf[c][:, :], in_=wstg[pr][:, 0:D]).then_inc(S["pool_h"], 1)

        @blk.tensor
        def _(tensor):
            for c in range(DHC):
                ps = psA if c % 2 == 0 else psB
                tensor.wait_ge(S["pool_h"], c + 1)
                if c >= 2:
                    tensor.wait_ge(S["act_h"], c - 1)
                for n in range(NLC):
                    for k in range(DC):
                        if c == 0 and n == 0:
                            tensor.wait_ge(S[f"xbs{k}"], 32)
                        inst = tensor.matmul(ps[:, NL * n:NL * (n + 1)],
                                             wg1bf[c][:, 128 * k:128 * (k + 1)],
                                             x1fm[k][:, 2 + NL * n:2 + NL * (n + 1)],
                                             start=(k == 0), stop=(k == DC - 1))
                        if k == DC - 1:
                            inst.then_inc(S["pe_h"], 1)

        @blk.scalar
        def _(scalar):
            for c in range(DHC):
                ps = psA if c % 2 == 0 else psB
                scalar.wait_ge(S["pe_h"], NLC * (c + 1))
                scalar.activation(out=hfm[c][:, :], in_=ps[:, 0:T], func=AF.Relu,
                                  bias=bg1_sb[:, c:c + 1]).then_inc(S["act_h"], 1)

    PE4 = NDC + DC
    VC = {}
    dv4 = [0]
    with nc.Block() as blk:
        @blk.sync
        def _(sync):
            for rc in range(RC):
                pr = rc % 2
                if rc >= 2:
                    sync.wait_ge(S["pool2"], 3 * (rc - 2) + 3)
                sync.dma_start(
                    out=wstg[pr][:, 0:D].rearrange("p (c i) -> p c i", c=DC),
                    in_=wuT[:, 128 * rc:128 * (rc + 1)].rearrange("(c p) i -> p c i", p=128),
                ).then_inc(S["wld" + str(pr)], 16)
                sync.dma_start(
                    out=wstg[pr][:, D:2 * D].rearrange("p (c i) -> p c i", c=DC),
                    in_=wvT[:, 128 * rc:128 * (rc + 1)].rearrange("(c p) i -> p c i", p=128),
                ).then_inc(S["wld" + str(pr)], 16)
                sync.dma_start(
                    out=wg2stg[pr][:, :].rearrange("p (c i) -> p c i", c=DHC),
                    in_=wg2T[:, 128 * rc:128 * (rc + 1)].rearrange("(c p) i -> p c i", p=128),
                ).then_inc(S["wld" + str(pr)], 16)
            sync.wait_ge(S["dve2"], 2 * (RC - 1) + 1)
            sync.dma_start(out=cc_in.rearrange("o (p c) -> (o p) c", p=128),
                           in_=S_sb[:, :]).then_inc(S["ccsp"], 16)

        @blk.tensor
        def _(tensor):
            for rc in range(RC):
                pr = rc % 2
                tensor.wait_ge(S["pool2"], 3 * rc + 3)
                for n in range(NLC):
                    if rc >= 1:
                        tensor.wait_ge(S["dve2v"], 4 * (rc - 1) + n + 1)
                    for k in range(DHC):
                        inst = tensor.matmul(psA[:, NL * n:NL * (n + 1)],
                                             wg2bf[pr][:, 128 * k:128 * (k + 1)],
                                             hfm[k][:, NL * n:NL * (n + 1)],
                                             start=(k == 0), stop=(k == DHC - 1))
                        if n == NLC - 1 and k == DHC - 1:
                            inst.then_inc(S["pe2"], 1)
                if rc >= 1:
                    tensor.wait_ge(S["dve2"], 2 * (rc - 1) + 1)
                for n in range(NLC):
                    for k in range(DC):
                        inst = tensor.matmul(psB[:, NL * n:NL * (n + 1)],
                                             wubf[pr][:, 128 * k:128 * (k + 1)],
                                             x1fm[k][:, 2 + NL * n:2 + NL * (n + 1)],
                                             start=(k == 0), stop=(k == DC - 1))
                        if n == NLC - 1 and k == DC - 1:
                            inst.then_inc(S["pe2"], 1)
                tensor.wait_ge(S["act2"], rc + 1)
                for n in range(NLC):
                    for k in range(DC):
                        inst = tensor.matmul(psA[:, NL * n:NL * (n + 1)],
                                             wvbf[pr][:, 128 * k:128 * (k + 1)],
                                             x1fm[k][:, 2 + NL * n:2 + NL * (n + 1)],
                                             start=(k == 0), stop=(k == DC - 1))
                        if n == NLC - 1 and k == DC - 1:
                            inst.then_inc(S["pe2"], 1)

        @blk.scalar
        def _(scalar):
            for rc in range(RC):
                pr = rc % 2
                scalar.wait_ge(S["pe2"], 3 * rc + 1)
                if rc >= 1:
                    scalar.wait_ge(S["dve2v"], 4 * (rc - 1) + 4)
                scalar.activation(out=g_sb[:, :], in_=psA[:, 0:T], func=AF.Sigmoid,
                                  bias=bg2_sb[:, rc:rc + 1]).then_inc(S["act2"], 1)
                scalar.wait_ge(S["dve2"], 2 * rc + 2)
                scalar.dma_start(out=C_d[rc, :, :], in_=C_sb[pr][:, :]).then_inc(S["sp2" + str(pr)], 16)
                scalar.wait_ge(S["dve2v"], 4 * rc + 4)
                scalar.dma_start(out=Vg_d[rc, :, :], in_=Vg_sb[pr][:, :]).then_inc(S["sp2" + str(pr)], 16)
                if 1 <= rc <= DC - 1:
                    scalar.wait_ge(S["dve3"], 3 * (rc - 1) + 3)
                    scalar.dma_start(out=t_d[rc - 1, :, :],
                                     in_=tfm0[:, :]).then_inc(S["tsp0"], 16)
            scalar.wait_ge(S["sp20"], 32 * ((RC + 1) // 2))
            scalar.wait_ge(S["sp21"], 32 * (RC // 2))

        @blk.gpsimd
        def _(gpsimd):
            gpsimd.wait_ge(S["ccsp"], 16)
            gpsimd.collective_compute(
                "AllGather", ALU.bypass,
                replica_groups=[list(range(NCORES))],
                ins=[cc_in[:, :]], outs=[cc_out[:, :]],
            ).then_inc(S["cc"])

        @blk.vector
        def _(vector):
            def cast_w(rc2):
                pr2 = rc2 % 2
                vector.wait_ge(S["wld" + str(pr2)], 48 * (rc2 // 2 + 1))
                if rc2 >= 2:
                    vector.wait_ge(S["pe2"], 3 * (rc2 - 2) + 3)
                vector.tensor_copy(out=wubf[pr2][:, :], in_=wstg[pr2][:, 0:D]).then_inc(S["pool2"], 1)
                vector.tensor_copy(out=wvbf[pr2][:, :], in_=wstg[pr2][:, D:2 * D]).then_inc(S["pool2"], 1)
                vector.tensor_copy(out=wg2bf[pr2][:, :], in_=wg2stg[pr2][:, :]).then_inc(S["pool2"], 1)
            cast_w(0)
            if RC > 1:
                cast_w(1)
            for rc in range(RC):
                pr = rc % 2
                vector.wait_ge(S["pe2"], 3 * rc + 2)
                vector.wait_ge(S["act2"], rc + 1)
                if rc >= 1:
                    vector.wait_ge(S["dve2"], 2 * rc)
                vector.scalar_tensor_tensor(out=ug_sb[:, :], in0=psB[:, 0:T],
                                            scalar=bu_sb[:, rc:rc + 1], in1=g_sb[:, :],
                                            op0=ALU.add, op1=ALU.mult,
                                            accum_out=S_sb[:, rc:rc + 1]).then_inc(S["dve2"], 1)
                if rc >= 2:
                    vector.wait_ge(S["sp2" + str(pr)], 32 * (rc // 2))
                vector.wait_ge(S["dve2"], 2 * rc + 1)
                vector.tensor_tensor_scan(out=C_sb[pr][:, :], data0=ones_sb[:, :],
                                          data1=ug_sb[:, :], initial=0.0,
                                          op0=ALU.mult, op1=ALU.add).then_inc(S["dve2"], 1)
                vector.wait_ge(S["pe2"], 3 * rc + 3)
                for j in range(NLC):
                    vector.scalar_tensor_tensor(out=Vg_sb[pr][:, NL * j:NL * (j + 1)],
                                                in0=psA[:, NL * j:NL * (j + 1)],
                                                scalar=bv_sb[:, rc:rc + 1],
                                                in1=g_sb[:, NL * j:NL * (j + 1)],
                                                op0=ALU.add, op1=ALU.mult).then_inc(S["dve2v"], 1)
                if rc + 2 < RC:
                    cast_w(rc + 2)
                c = rc
                if c < DC:
                    if c >= 1:
                        vector.wait_ge(S["tsp0"], 16 * c)
                    vector.tensor_scalar(out=tfm0[:, :], in0=x1fm[c][:, 1:T + 1],
                                         scalar1=convw_sb[:, 3 * c:3 * c + 1],
                                         scalar2=convb_sb[:, c:c + 1],
                                         op0=ALU.mult, op1=ALU.add).then_inc(S["dve3"], 1)
                    vector.wait_ge(S["dve3"], 3 * c + 1)
                    vector.scalar_tensor_tensor(out=tfm0[:, :], in0=x1fm[c][:, 2:T + 2],
                                                scalar=convw_sb[:, 3 * c + 1:3 * c + 2],
                                                in1=tfm0[:, :], op0=ALU.mult, op1=ALU.add).then_inc(S["dve3"], 1)
                    vector.wait_ge(S["dve3"], 3 * c + 2)
                    vector.scalar_tensor_tensor(out=tfm0[:, :], in0=x1fm[c][:, 3:T + 3],
                                                scalar=convw_sb[:, 3 * c + 2:3 * c + 3],
                                                in1=tfm0[:, :], op0=ALU.mult, op1=ALU.add).then_inc(S["dve3"], 1)

        @blk.gpsimd
        def _(gpsimd):
            def ld_t(lt):
                pr = lt % 2
                if lt >= 2:
                    gpsimd.wait_ge(S["pe4"], PE4 * (lt - 2) + PE4)
                if lt < 2:
                    gpsimd.wait_ge(S["tsp0"], 16 * DC)
                gpsimd.dma_start(
                    out=tin[pr][:, :].rearrange("p (dc l) -> p dc l", dc=DC),
                    in_=t_d[:, :, 128 * lt:128 * (lt + 1)].rearrange("dc p l -> p dc l"),
                ).then_inc(S["tld" + str(pr)], 16)

            gpsimd.wait_ge(S["pe2"], 3 * RC)
            ld_t(0)
            if LT > 1:
                ld_t(1)
            for ch in range(RC):
                prc = ch % 2
                if ch == 0:
                    gpsimd.wait_ge(S["pool2"], 3 * RC)
                if ch >= 2:
                    gpsimd.wait_ge(S["pool3"], ch - 1)
                gpsimd.dma_start(out=wun_stg[prc][:, :],
                                 in_=wuT[128 * ch:128 * (ch + 1), :]).then_inc(S["wuld" + str(prc)], 16)
            for lt in range(2, LT):
                ld_t(lt)
                if lt == min(5, LT - 1):
                    gpsimd.dma_start(
                        out=gstg[0][:, :].rearrange("p (c i) -> p c i", c=DC),
                        in_=gateT[:, 0:128].rearrange("(c p) i -> p c i", p=128),
                    ).then_inc(S["gld0e"], 16)
                    gpsimd.dma_start(
                        out=ustg[0][:, :].rearrange("p (c i) -> p c i", c=DC),
                        in_=upT[:, 0:128].rearrange("(c p) i -> p c i", p=128),
                    ).then_inc(S["gld0e"], 16)

        @blk.vector
        def _(vector):
            def wun_cast(ch):
                prc = ch % 2
                vector.wait_ge(S["wuld" + str(prc)], 16 * (ch // 2 + 1))
                vector.tensor_copy(out=wun_bf[ch][:, :], in_=wun_stg[prc][:, :]).then_inc(S["pool3"], 1)

            vector.wait_ge(S["pe2"], 3 * RC)
            vector.wait_ge(S["sp20"], 32 * ((RC + 1) // 2))
            vector.wait_ge(S["sp21"], 32 * (RC // 2))
            wun_cast(0)
            if RC > 1:
                wun_cast(1)
            vector.wait_ge(S["agld"], 16)
            for r in range(NCORES):
                if r >= 1:
                    vector.wait_ge(S["dve3b"], r)
                vector.scalar_tensor_tensor(out=O_sb[:, :], in0=ag_sb[:, RC * r:RC * (r + 1)],
                                            scalar=pm_sb[:, r:r + 1], in1=O_sb[:, :],
                                            op0=ALU.mult, op1=ALU.add).then_inc(S["dve3b"], 1)
            vector.wait_ge(S["dve3b"], NCORES)

            def inc(inst, tag):
                dv4[0] += 1
                VC[tag] = dv4[0]
                inst.then_inc(S["dve4"], 1)

            def emit_G(lt):
                p3 = lt % 3
                pr = lt % 2
                vector.wait_ge(S["ld4a" + str(p3)], 16 * (lt // 3 + 1))
                vector.wait_ge(S["ld4v" + str(p3)], 16 * (lt // 3 + 1))
                if lt >= 2:
                    vector.wait_ge(S["pe4"], PE4 * (lt - 2) + NDC)
                for rc in range(RC):
                    inc(vector.scalar_tensor_tensor(
                        out=G_sb[pr][:, 128 * rc:128 * (rc + 1)],
                        in0=Cin[p3][:, 128 * rc:128 * (rc + 1)],
                        scalar=O_sb[:, rc:rc + 1],
                        in1=Vgin[p3][:, 128 * rc:128 * (rc + 1)],
                        op0=ALU.add, op1=ALU.mult), ("gmc", lt, rc))
                VC[("gm", lt)] = VC[("gmc", lt, RC - 1)]

            emit_G(0)
            if LT > 1:
                emit_G(1)
            for ch in range(2, RC):
                wun_cast(ch)
            for lt in range(LT):
                pr = lt % 2
                vector.wait_ge(S["pe4"], PE4 * lt + PE4)
                vector.wait_ge(S["ld4b" + str(pr)], 16 * (lt // 2 + 1))
                if lt >= 1:
                    vector.wait_ge(S["sp4" + str((lt - 1) % 2)], 32 * ((lt - 1) // 2 + 1))
                inc(vector.scalar_tensor_tensor(out=xp_sb[:, :], in0=(psA if pr == 0 else psB)[:, 0:D],
                                                scalar=1.0, in1=xin4[pr][:, :], op0=ALU.mult,
                                                op1=ALU.add), ("xp", lt))
                if lt + 2 < LT:
                    emit_G(lt + 2)
                vector.wait_ge(S["dve4"], VC[("xp", lt)])
                inc(vector.reduce_sum(stat[:, 0:1], xp_sb[:, :], axis=AXX), ("red", lt))
                vector.wait_ge(S["act4"], 3 * lt + 3)
                inc(vector.reciprocal(stat[:, 4:5], stat[:, 3:4]), ("rec", lt))
                vector.wait_ge(S["dve4"], VC[("rec", lt)])
                inc(vector.tensor_mul(stat[:, 5:6], stat[:, 1:2], stat[:, 4:5]), ("nb", lt))
                if lt >= 2:
                    vector.wait_ge(S["sp4" + str(pr)], 32 * (lt // 2))
                vector.wait_ge(S["dve4"], VC[("nb", lt)])
                inc(vector.tensor_scalar(out=x2tm[pr][:, :], in0=xp_sb[:, :],
                                         scalar1=stat[:, 4:5], scalar2=stat[:, 5:6],
                                         op0=ALU.mult, op1=ALU.add), ("ap", lt))

        @blk.scalar
        def _(scalar):
            scalar.wait_ge(S["dve3"], 3 * DC)
            scalar.dma_start(out=t_d[DC - 1, :, :],
                             in_=tfm0[:, :]).then_inc(S["tsp0"], 16)

            def ld_v(lt3):
                p3 = lt3 % 3
                if lt3 >= 3:
                    scalar.wait_ge(S["dve4"], VC[("gm", lt3 - 3)])
                scalar.dma_start(
                    out=Vgin[p3][:, :].rearrange("p (rc l) -> p rc l", rc=RC),
                    in_=Vg_d[:, :, 128 * lt3:128 * (lt3 + 1)].rearrange("rc p l -> p rc l"),
                ).then_inc(S["ld4v" + str(p3)], 16)

            def ld_b(lt2):
                pr2 = lt2 % 2
                if lt2 >= 2:
                    scalar.wait_ge(S["act4"], 3 * (lt2 - 2) + 2)
                scalar.dma_start(out=xin4[pr2][:, :],
                                 in_=x_pad[2 + 128 * lt2:2 + 128 * (lt2 + 1), :]).then_inc(S["ld4b" + str(pr2)], 16)

            scalar.wait_ge(S["pe2"], 3 * RC)
            ld_v(0)
            ld_b(0)
            if LT > 1:
                ld_v(1)
                ld_b(1)
            if LT > 2:
                ld_v(2)
            scalar.wait_ge(S["cc"], 1)
            scalar.dma_start(
                out=ag_sb[:, 0:NCORES * RC].rearrange("p (r c) -> p r c", r=NCORES),
                in_=cc_out[:, :].rearrange("r (p c) -> p r c", p=128),
            ).then_inc(S["agld"], 16)
            for lt in range(LT):
                pr = lt % 2
                if lt == min(NQ // 128, LT - 1):
                    nq = NQ // 128
                    scalar.wait_ge(S["sp40"], 32 * ((nq + 1) // 2))
                    scalar.wait_ge(S["sp41"], 32 * (nq // 2))
                    scalar.wait_ge(S["tsp0"], 16 * DC)
                    for c in range(DC):
                        scalar.dma_start(out=x2q0[c][:, :],
                                         in_=x2tm_d[0:NQ, 128 * c:128 * (c + 1)],
                                         transpose=True).then_inc(S["xbq0"], 16)
                scalar.wait_ge(S["dve4"], VC[("red", lt)])
                if lt >= 1:
                    scalar.wait_ge(S["act4"], 3 * lt)
                scalar.mul(stat[:, 1:2], stat[:, 0:1], -invD).then_inc(S["act4"], 1)
                scalar.wait_ge(S["act4"], 3 * lt + 1)
                scalar.activation(out=xin4[pr][:, :], in_=xp_sb[:, :], func=AF.Square,
                                  bias=stat[:, 1:2], accum_out=stat[:, 2:3]).then_inc(S["act4"], 1)
                scalar.wait_ge(S["act4"], 3 * lt + 2)
                scalar.activation(out=stat[:, 3:4], in_=stat[:, 2:3], func=AF.Sqrt,
                                  scale=cvals[:, 1:2], bias=cvals[:, 0:1]).then_inc(S["act4"], 1)
                if lt + 2 < LT:
                    ld_b(lt + 2)
                if lt + 3 < LT:
                    ld_v(lt + 3)
                scalar.wait_ge(S["dve4"], VC[("ap", lt)])
                scalar.dma_start(out=xp_d[128 * lt:128 * (lt + 1), :],
                                 in_=xp_sb[:, :]).then_inc(S["sp4" + str(pr)], 16)
                scalar.dma_start(out=x2tm_d[128 * lt:128 * (lt + 1), :],
                                 in_=x2tm[pr][:, :]).then_inc(S["sp4" + str(pr)], 16)
            scalar.wait_ge(S["sp40"], 32 * ((LT + 1) // 2))
            scalar.wait_ge(S["sp41"], 32 * (LT // 2))

        @blk.sync
        def _(sync):
            def ld_a(lt):
                p3 = lt % 3
                if lt == 0:
                    sync.wait_ge(S["pe2"], 3 * RC)
                    sync.wait_ge(S["sp20"], 32 * ((RC + 1) // 2))
                    sync.wait_ge(S["sp21"], 32 * (RC // 2))
                if lt >= 3:
                    sync.wait_ge(S["dve4"], VC[("gm", lt - 3)])
                sync.dma_start(
                    out=Cin[p3][:, :].rearrange("p (rc l) -> p rc l", rc=RC),
                    in_=C_d[:, :, 128 * lt:128 * (lt + 1)].rearrange("rc p l -> p rc l"),
                ).then_inc(S["ld4a" + str(p3)], 16)

            for lt in range(LT):
                ld_a(lt)


        @blk.tensor
        def _(tensor):
            for lt in range(LT):
                pr = lt % 2
                ps = psA if pr == 0 else psB
                if lt == 0:
                    tensor.wait_ge(S["dve2v"], 4 * RC)
                if lt == 1:
                    tensor.wait_ge(S["dve2"], 2 * RC - 1)
                if lt >= 2:
                    tensor.wait_ge(S["dve4"], VC[("xp", lt - 2)])
                for rc in range(RC):
                    tensor.wait_ge(S["dve4"], VC[("gmc", lt, rc)])
                    if lt == 0:
                        tensor.wait_ge(S["pool3"], rc + 1)
                    for n in range(NDC):
                        inst = tensor.matmul(ps[:, ND * n:ND * (n + 1)],
                                             G_sb[pr][:, 128 * rc:128 * (rc + 1)],
                                             wun_bf[rc][:, ND * n:ND * (n + 1)],
                                             start=(rc == 0), stop=False)
                        if rc == RC - 1:
                            inst.then_inc(S["pe4"], 1)
                tensor.wait_ge(S["tld" + str(pr)], 16 * (lt // 2 + 1))
                for dc in range(DC):
                    tensor.matmul(ps[:, 128 * dc:128 * (dc + 1)],
                                  tin[pr][:, 128 * dc:128 * (dc + 1)],
                                  ident[:, :], is_transpose=True,
                                  start=False, stop=((dc + 1) % 4 == 0 or dc == DC - 1)
                                  ).then_inc(S["pe4"], 1)

        @blk.sync
        def _(sync):
            sync.wait_ge(S["pe4"], PE4 * LT)
            sync.wait_ge(S["sp40"], 32 * ((LT + 1) // 2))
            sync.wait_ge(S["sp41"], 32 * (LT // 2))
            sync.dma_start(
                out=gstg[1][:, :].rearrange("p (c i) -> p c i", c=DC),
                in_=gateT[:, 128:256].rearrange("(c p) i -> p c i", p=128),
            ).then_inc(S["gld1"], 16)
            sync.dma_start(
                out=ustg[1][:, :].rearrange("p (c i) -> p c i", c=DC),
                in_=upT[:, 128:256].rearrange("(c p) i -> p c i", p=128),
            ).then_inc(S["gld1"], 16)
            for q in range(1, NQC):
                sem = [None, S["xbq1"], S["xb123"], S["xbq3"]][q]
                for c in range(DC):
                    sync.dma_start(out=x2fm[c][:, NQ * q:NQ * (q + 1)],
                                   in_=x2tm_d[NQ * q:NQ * (q + 1), 128 * c:128 * (c + 1)],
                                   transpose=True).then_inc(sem, 16)

        @blk.sync
        def _(sync):
            for ic in range(2, IC):
                pr = ic % 2
                sync.wait_ge(S["pool5"], 2 * (ic - 2) + 2)
                sync.dma_start(
                    out=gstg[pr][:, :].rearrange("p (c i) -> p c i", c=DC),
                    in_=gateT[:, 128 * ic:128 * (ic + 1)].rearrange("(c p) i -> p c i", p=128),
                ).then_inc(S["gld" + str(pr)], 16)
                sync.dma_start(
                    out=ustg[pr][:, :].rearrange("p (c i) -> p c i", c=DC),
                    in_=upT[:, 128 * ic:128 * (ic + 1)].rearrange("(c p) i -> p c i", p=128),
                ).then_inc(S["gld" + str(pr)], 16)

        @blk.tensor
        def _(tensor):
            def x2src(k, n):
                if n == 0:
                    return x2q0[k][:, :]
                return x2fm[k][:, NL * n:NL * (n + 1)]

            for ic in range(IC):
                pr = ic % 2
                tensor.wait_ge(S["pool5"], 2 * ic + 2)
                if ic >= 1:
                    tensor.wait_ge(S["act5"], ic)
                for n in range(NLC):
                    if ic == 0:
                        sem = [S["xbq0"], S["xbq1"], S["xb123"], S["xbq3"]][n]
                        tensor.wait_ge(sem, 16 * DC)
                    for k in range(DC):
                        inst = tensor.matmul(psA[:, NL * n:NL * (n + 1)],
                                             gbf[pr][:, 128 * k:128 * (k + 1)],
                                             x2src(k, n),
                                             start=(k == 0), stop=(k == DC - 1))
                        if n == NLC - 1 and k == DC - 1:
                            inst.then_inc(S["pe5"], 1)
                if ic >= 1:
                    tensor.wait_ge(S["dve5"], ic)
                for n in range(NLC):
                    for k in range(DC):
                        inst = tensor.matmul(psB[:, NL * n:NL * (n + 1)],
                                             ubf[pr][:, 128 * k:128 * (k + 1)],
                                             x2src(k, n),
                                             start=(k == 0), stop=(k == DC - 1))
                        if n == NLC - 1 and k == DC - 1:
                            inst.then_inc(S["pe5"], 1)

        @blk.scalar
        def _(scalar):
            for ic in range(IC):
                pr = ic % 2
                scalar.wait_ge(S["pe5"], 2 * ic + 1)
                if ic >= 1:
                    scalar.wait_ge(S["dve5"], ic)
                scalar.activation(out=silu_sb[:, :], in_=psA[:, 0:T], func=AF.Silu,
                                  scale=gs_sb[:, ic:ic + 1]).then_inc(S["act5"], 1)
                if ic >= 1:
                    scalar.dma_start(out=hf_d[ic - 1, :, :],
                                     in_=hf_sb[(ic - 1) % 2][:, :]).then_inc(S["hsp" + str((ic - 1) % 2)], 16)
            scalar.wait_ge(S["dve5"], IC)
            scalar.dma_start(out=hf_d[IC - 1, :, :],
                             in_=hf_sb[(IC - 1) % 2][:, :]).then_inc(S["hsp" + str((IC - 1) % 2)], 16)
            scalar.wait_ge(S["hsp0"], 16 * ((IC + 1) // 2))
            scalar.wait_ge(S["hsp1"], 16 * (IC // 2))

        @blk.vector
        def _(vector):
            def deq(ic2):
                pr2 = ic2 % 2
                if ic2 == 0:
                    vector.wait_ge(S["gld0e"], 32)
                elif pr2 == 0:
                    vector.wait_ge(S["gld0"], 32 * (ic2 // 2))
                else:
                    vector.wait_ge(S["gld1"], 32 * (ic2 // 2 + 1))
                if ic2 >= 2:
                    vector.wait_ge(S["pe5"], 2 * (ic2 - 2) + 2)
                vector.tensor_scalar(out=gbf[pr2][:, :], in0=gstg[pr2][:, :], scalar1=-128.0,
                                     scalar2=None, op0=ALU.add).then_inc(S["pool5"], 1)
                vector.tensor_scalar(out=ubf[pr2][:, :], in0=ustg[pr2][:, :], scalar1=-128.0,
                                     scalar2=None, op0=ALU.add).then_inc(S["pool5"], 1)
            vector.wait_ge(S["sp40"], 32 * ((LT + 1) // 2))
            vector.wait_ge(S["sp41"], 32 * (LT // 2))
            deq(0)
            if IC > 1:
                deq(1)
            for ic in range(IC):
                pr = ic % 2
                if ic + 2 < IC:
                    deq(ic + 2)
                vector.wait_ge(S["pe5"], 2 * ic + 2)
                vector.wait_ge(S["act5"], ic + 1)
                if ic >= 2:
                    vector.wait_ge(S["hsp" + str(pr)], 16 * (ic // 2))
                vector.scalar_tensor_tensor(out=hf_sb[pr][:, :], in0=psB[:, 0:T],
                                            scalar=us_sb[:, ic:ic + 1], in1=silu_sb[:, :],
                                            op0=ALU.mult, op1=ALU.mult).then_inc(S["dve5"], 1)

        @blk.sync
        def _(sync):
            hg = [S["hga"], S["hgb"]]
            sync.wait_ge(S["hsp0"], 16 * ((IC + 1) // 2))
            sync.wait_ge(S["hsp1"], 16 * (IC // 2))
            for g in range(IC // 8):
                if g >= 2:
                    sync.wait_ge(hg[g % 2], 128 * (g // 2))
                for j in range(8):
                    ic = 8 * g + j
                    sync.dma_start(out=hfh[ic][:, :],
                                   in_=hf_d[ic, :, 0:TH]).then_inc(hg[g % 2], 16)

        @blk.gpsimd
        def _(gpsimd):
            gpsimd.wait_ge(S["pe4"], PE4 * LT)
            for lh in range(LH):
                for dc in range(NDC):
                    for ic in range(IC):
                        seq = (lh * NDC + dc) * IC + ic
                        pr = seq % 4
                        if seq >= 4:
                            gpsimd.wait_ge(S["pool6"], seq - 3)
                        gpsimd.dma_start(out=dstg[pr][:, :],
                                         in_=downT[128 * ic:128 * (ic + 1),
                                                   ND * dc:ND * (dc + 1)]).then_inc(S["dld" + str(pr)], 16)

        @blk.tensor
        def _(tensor):
            hg = [S["hga"], S["hgb"]]
            for lh in range(LH):
                for dc in range(NDC):
                    dcseq = lh * NDC + dc
                    for ic in range(IC):
                        if dc == 0 and ic % 8 == 0:
                            g = lh * (IC // 8) + ic // 8
                            tensor.wait_ge(hg[g % 2], 128 * (g // 2 + 1))
                        seq = dcseq * IC + ic
                        pr = seq % 4
                        tensor.wait_ge(S["pool6"], seq + 1)
                        last = None
                        for lt in range(LTH):
                            if ic == 0 and dcseq >= 1:
                                tensor.wait_ge(S["dve6"], (dcseq - 1) * 2 * LTH + 2 * (lt + 1))
                            ps = psall[lt // 4] if LTH > 4 else psall[0]
                            slc = ps[:, 512 * (lt % 4):512 * (lt % 4) + ND]
                            last = tensor.matmul(slc,
                                                 hfh[ic][:, 128 * lt:128 * (lt + 1)],
                                                 dbf[pr][:, :],
                                                 start=(ic == 0), stop=(ic == IC - 1))
                        last.then_inc(S["pe6"], 1)

        @blk.scalar
        def _(scalar):
            for lh in range(LH):
                for dc in range(NDC):
                    dcseq = lh * NDC + dc
                    pr = dcseq % 2
                    if dcseq >= 2:
                        scalar.wait_ge(S["dve6"], (dcseq - 1) * 2 * LTH)
                    scalar.dma_start(
                        out=xr_sb[pr][:, :].rearrange("p (lt d) -> p lt d", lt=LTH),
                        in_=xp_d[TH * lh:TH * (lh + 1), ND * dc:ND * (dc + 1)].rearrange("(lt p) d -> p lt d", p=128),
                    ).then_inc(S["xrld" if pr == 0 else "xrld1"], 16)
                    if lh == 0 and dc == NDC - 1:
                        hg = [S["hga"], S["hgb"]]
                        for ic in range(IC):
                            if ic % 8 == 0:
                                g = (IC // 8) + ic // 8
                                scalar.wait_ge(hg[g % 2], 128 * (g // 2))
                            scalar.wait_ge(S["pe6"], (NDC - 1) * IC + ic + 1)
                            scalar.dma_start(out=hfh[ic][:, :],
                                             in_=hf_d[ic, :, TH:2 * TH]).then_inc(hg[g % 2], 16)
                    for lt in range(LTH):
                        oseq = dcseq * LTH + lt
                        scalar.wait_ge(S["dve6"], dcseq * 2 * LTH + 2 * (lt + 1))
                        scalar.dma_start(
                            out=out_d[TH * lh + 128 * lt:TH * lh + 128 * (lt + 1),
                                      ND * dc:ND * (dc + 1)],
                            in_=outb[oseq % 4][:, :],
                        ).then_inc(S["osp" + str(oseq % 4)], 16)

        @blk.vector
        def _(vector):
            def deq6(seq):
                pr4 = seq % 4
                vector.wait_ge(S["dld" + str(pr4)], 16 * (seq // 4 + 1))
                if seq >= 4:
                    vector.wait_ge(S["pe6"], seq - 3)
                vector.tensor_scalar(out=dbf[pr4][:, :], in0=dstg[pr4][:, :], scalar1=-128.0,
                                     scalar2=None, op0=ALU.add).then_inc(S["pool6"], 1)

            def evacs(dcseq):
                dc = dcseq % NDC
                pr = dcseq % 2
                vector.wait_ge(S["xrld" if pr == 0 else "xrld1"], 16 * (dcseq // 2 + 1))
                vector.wait_ge(S["pe6"], (dcseq + 1) * IC)
                for lt in range(LTH):
                    oseq = dcseq * LTH + lt
                    ps = psall[lt // 4] if LTH > 4 else psall[0]
                    slc = ps[:, 512 * (lt % 4):512 * (lt % 4) + ND]
                    ob = outb[oseq % 4]
                    if oseq >= 4:
                        vector.wait_ge(S["osp" + str(oseq % 4)], 16 * (oseq // 4))
                    vector.tensor_mul(out=ob[:, :], in0=slc,
                                      in1=sd_sb[:, ND * dc:ND * (dc + 1)]).then_inc(S["dve6"], 1)
                    vector.wait_ge(S["dve6"], dcseq * 2 * LTH + 2 * lt + 1)
                    vector.tensor_add(out=ob[:, :], in0=ob[:, :],
                                      in1=xr_sb[pr][:, lt * ND:(lt + 1) * ND]).then_inc(S["dve6"], 1)

            NSEQ = LH * NDC
            for dcseq in range(NSEQ):
                for ic in range(4):
                    deq6(dcseq * IC + ic)
                if dcseq >= 1:
                    evacs(dcseq - 1)
                for ic in range(4, IC):
                    deq6(dcseq * IC + ic)
            evacs(NSEQ - 1)

        @blk.sync
        def _(sync):
            total = LH * NDC * LTH
            for j in range(4):
                sync.wait_ge(S["osp" + str(j)], 16 * ((total - j + 3) // 4))

    return nc


def _prep_inputs(L0_attn_x, L0_attn_Wu, L0_attn_bu, L0_attn_Wv, L0_attn_bv,
                 L0_attn_Wg1, L0_attn_bg1, L0_attn_Wg2, L0_attn_bg2,
                 L0_attn_conv_w, L0_attn_conv_b, L0_ffn_gate_w, L0_ffn_gate_scale,
                 L0_ffn_up_w, L0_ffn_up_scale, L0_ffn_down_w, L0_ffn_down_scale,
                 cfg=FULL, ncores=NCORES):
    T, D, R, DH, I = cfg["T"], cfg["D"], cfg["R"], cfg["DH"], cfg["I"]
    TP = T + 128
    RCn, DHCn, DCn, ICn = R // 128, DH // 128, D // 128, I // 128
    x = np.asarray(L0_attn_x, dtype=np.float32).reshape(-1, D)
    Ltot = x.shape[0]
    f32c = lambda a: np.ascontiguousarray(np.asarray(a, dtype=np.float32))
    u8c = lambda a: np.ascontiguousarray(np.asarray(a).astype(np.uint8))

    shared = dict(
        wuT=f32c(np.asarray(L0_attn_Wu).T),
        wvT=f32c(np.asarray(L0_attn_Wv).T),
        wg1T=f32c(np.asarray(L0_attn_Wg1).T),
        wg2T=f32c(np.asarray(L0_attn_Wg2).T),
        bu_r=f32c(np.asarray(L0_attn_bu).reshape(RCn, 128).T),
        bv_r=f32c(np.asarray(L0_attn_bv).reshape(RCn, 128).T),
        bg1_r=f32c(np.asarray(L0_attn_bg1).reshape(DHCn, 128).T),
        bg2_r=f32c(np.asarray(L0_attn_bg2).reshape(RCn, 128).T),
        convw_r=f32c(np.asarray(L0_attn_conv_w).reshape(D, 3).reshape(DCn, 128, 3)
                     .transpose(1, 0, 2).reshape(128, 3 * DCn)),
        convb_r=f32c(np.asarray(L0_attn_conv_b).reshape(DCn, 128).T),
        gateT=u8c(np.asarray(L0_ffn_gate_w).T),
        upT=u8c(np.asarray(L0_ffn_up_w).T),
        downT=u8c(np.asarray(L0_ffn_down_w).T),
        gs_r=f32c(np.asarray(L0_ffn_gate_scale).reshape(ICn, 128).T),
        us_r=f32c(np.asarray(L0_ffn_up_scale).reshape(ICn, 128).T),
        sd_b=f32c(np.broadcast_to(np.asarray(L0_ffn_down_scale, dtype=np.float32), (128, D))),
        ident_in=np.eye(128, dtype=np.float32),
    )
    in_maps = []
    Tcore = Ltot // ncores
    assert Tcore == T, (Tcore, T)
    for core in range(ncores):
        s = core * T
        xp = np.zeros((TP, D), dtype=np.float32)
        lo = max(s - 1, 0)
        hi = min(s + T + 1, Ltot)
        xp[2 + (lo - s):2 + (hi - s), :] = x[lo:hi]
        pm = np.zeros((128, ncores), dtype=np.float32)
        pm[:, :core] = 1.0
        m = dict(shared)
        m["x_pad"] = xp
        m["premask"] = pm
        in_maps.append(m)
    return in_maps


def kernel(**inputs):
    cfg = FULL
    in_maps = _prep_inputs(cfg=cfg, ncores=NCORES, **inputs)
    nc = build_nc(**cfg)
    res = run_bass_kernel_spmd(nc, in_maps, core_ids=list(range(NCORES)))
    out = np.concatenate([r["out_d"] for r in res.results], axis=0)
    return out.reshape(1, NCORES * cfg["T"], cfg["D"]).astype(np.float32)
